# version 31
# baseline (speedup 1.0000x reference)
import contextlib
import numpy as np
import ml_dtypes
import concourse.bass as bass
import concourse.mybir as mybir
from concourse.bass_utils import run_bass_kernel_spmd

F32 = mybir.dt.float32
BF16 = mybir.dt.bfloat16
I32 = mybir.dt.int32
AF = mybir.ActivationFunctionType
ALU = mybir.AluOpType

P = 128
C = 1024
H = 3072
CC = C // P
HC = H // P
EPS = 1e-5
INV127 = 1.0 / 127.0
INV127SQ = INV127 * INV127

ENGINES = ("tensor", "vector", "scalar", "gpsimd", "sync")
ENG_SEM = {"tensor": "pe", "vector": "dve", "scalar": "act", "gpsimd": "pool"}


class Res:
    __slots__ = ("writer", "readers", "name")

    def __init__(self, name):
        self.writer = None
        self.readers = []
        self.name = name


class Build:

    def __init__(self, nc):
        self.nc = nc
        self.q = {e: [] for e in ENGINES}
        self.sems = {}
        self.semv = {}
        self.waited = {e: {} for e in ENGINES}

    def add_sem(self, name, handle):
        self.sems[name] = handle
        self.semv[name] = 0

    def op(self, eng, fn, reads=(), writes=(), dma_sem=None):
        edges = {}

        def add(t):
            if t is not None:
                s, v = t
                if edges.get(s, 0) < v:
                    edges[s] = v

        for r in reads:
            add(r.writer)
        for w in writes:
            add(w.writer)
            for t in w.readers:
                add(t)
        if dma_sem is not None:
            add((dma_sem, self.semv[dma_sem]))
        wl = []
        for s, v in sorted(edges.items()):
            if self.waited[eng].get(s, 0) >= v:
                continue
            wl.append((s, v))
            self.waited[eng][s] = v
        sem = dma_sem if dma_sem is not None else ENG_SEM[eng]
        inc = 16 if dma_sem is not None else 1
        self.semv[sem] += inc
        tick = (sem, self.semv[sem])

        def emit(e, wl=wl, fn=fn, sem=sem, inc=inc):
            for s, v in wl:
                e.wait_ge(self.sems[s], v)
            inst = fn(e)
            inst.then_inc(self.sems[sem], inc)

        self.q[eng].append(emit)
        for w in writes:
            w.writer = tick
            w.readers = []
        for r in reads:
            r.readers.append(tick)
        return tick

    def raw_wait(self, eng, sem, val):
        def emit(e, sem=sem, val=val):
            e.wait_ge(self.sems[sem], val)
        self.q[eng].append(emit)

    def fence(self, engs=ENGINES):
        for eng in engs:
            for s, v in list(self.semv.items()):
                if v == 0:
                    continue
                if self.waited[eng].get(s, 0) >= v:
                    continue
                self.raw_wait(eng, s, v)
                self.waited[eng][s] = v

    def finalize(self, block):
        for eng in ENGINES:
            ops = self.q[eng]
            if not ops:
                continue
            section = getattr(block, eng)

            def body(e, ops=ops):
                for f in ops:
                    f(e)
            section(body)
            self.q[eng] = []


def dram_ap(handle, offset, dims):
    return bass.AP(handle, offset, [list(d) for d in dims])


def build_program(T=4096):
    TT1 = 256
    NC1 = T // TT1
    TS1 = TT1 // P
    TT2 = 512
    NC2 = T // TT2
    TS2 = TT2 // P

    nc = bass.Bass("TRN2", target_bir_lowering=False)

    x_d = nc.dram_tensor("x", [T, C], F32, kind="ExternalInput")
    wk_d = nc.dram_tensor("Wk", [C, C], F32, kind="ExternalInput")
    wr_d = nc.dram_tensor("Wr", [C, C], F32, kind="ExternalInput")
    wo_d = nc.dram_tensor("Wo", [C, C], F32, kind="ExternalInput")
    wg_d = nc.dram_tensor("wg", [H, C], I32, kind="ExternalInput")
    wu_d = nc.dram_tensor("wu", [H, C], I32, kind="ExternalInput")
    wd_d = nc.dram_tensor("wd", [C, H], I32, kind="ExternalInput")
    tdp_d = nc.dram_tensor("tdp", [P, CC], F32, kind="ExternalInput")
    tfp_d = nc.dram_tensor("tfp", [P, CC], F32, kind="ExternalInput")
    id_d = nc.dram_tensor("ident", [P, P], BF16, kind="ExternalInput")
    out_d = nc.dram_tensor("out", [T, C], F32, kind="ExternalOutput")

    wgu_s = nc.dram_tensor("wgu_s", [HC, 2, CC, P, P], BF16, kind="Internal")
    wdT_s = nc.dram_tensor("wdT_s", [HC, P, C], BF16, kind="Internal")
    xn_s = nc.dram_tensor("xn_s", [CC, P, T], BF16, kind="Internal")
    outp_s = nc.dram_tensor("outp_s", [CC, P, T], BF16, kind="Internal")

    bd = Build(nc)

    with contextlib.ExitStack() as top:
        E = top.enter_context
        for s in ("pe", "dve", "act", "pool"):
            bd.add_sem(s, E(nc.semaphore(s)))
        dma_sems = [
            "d_setup",
            "d_x0", "d_x1",
            "d_xns0", "d_xns1",
            "d_ots0", "d_ots1",
            "d_pp0", "d_pp1",
            "d_pps0", "d_pps1",
            "d_xnc0", "d_xnc1",
            "d_otc0", "d_otc1",
            "d_x20", "d_x21",
            "d_wgu0", "d_wgu1", "d_wgu2", "d_wgu3",
            "d_wd0", "d_wd1", "d_wd2", "d_wd3",
            "d_rs0", "d_rs1",
        ]
        for s in dma_sems:
            bd.add_sem(s, E(nc.semaphore(s)))

        block = E(nc.Block())

        ident = E(nc.sbuf_tensor("identsb", [P, P], BF16))
        woT = E(nc.sbuf_tensor("woT", [P, CC * C], BF16))
        cst = E(nc.sbuf_tensor("cst", [P, 8], F32))
        euv = E(nc.sbuf_tensor("euv", [P, CC], F32))
        dv = E(nc.sbuf_tensor("dv", [P, CC], F32))
        tdv = E(nc.sbuf_tensor("tdv", [P, CC], F32))
        tfv = E(nc.sbuf_tensor("tfv", [P, CC], F32))
        wneg = E(nc.sbuf_tensor("wneg", [P, CC], F32))

        r_ident = Res("ident")
        r_woT = Res("woT")
        r_cst = Res("cst")
        r_euv = Res("euv")
        r_dv = Res("dv")
        r_tdv = Res("tdv")
        r_tfv = Res("tfv")
        r_wneg = Res("wneg")
        r_wgu_s = [Res(f"wgu_s{hc}") for hc in range(HC)]
        r_wdT_s = [Res(f"wdT_s{hc}") for hc in range(HC)]
        r_xn_s = [Res(f"xn_s{ch}") for ch in range(NC1)]
        r_out_s = [Res(f"out_s{ch}") for ch in range(NC1)]

        with contextlib.ExitStack() as ph1:
            E1 = ph1.enter_context
            wkT = E1(nc.sbuf_tensor("wkT", [P, CC * C], BF16))
            wrT = E1(nc.sbuf_tensor("wrT", [P, CC * C], BF16))
            x_tok = [E1(nc.sbuf_tensor(f"x_tok{i}", [P, TS1 * C], F32)) for i in range(2)]
            xn_tok = [E1(nc.sbuf_tensor("xn_tok0", [P, TS1 * C], BF16))]
            xn_fm = [E1(nc.sbuf_tensor(f"xn_fm{i}", [P, CC * TT1], BF16)) for i in range(2)]
            outb = [E1(nc.sbuf_tensor("outb0", [P, CC * TT1], BF16))]
            S1 = [E1(nc.sbuf_tensor(f"S1_{i}", [P, CC * TT1], F32)) for i in range(2)]
            S3 = [E1(nc.sbuf_tensor(f"S3_{i}", [P, CC * TT1], F32)) for i in range(2)]
            S4 = [E1(nc.sbuf_tensor(f"S4_{i}", [P, CC * TT1], F32)) for i in range(2)]
            S5 = [E1(nc.sbuf_tensor(f"S5_{i}", [P, CC * TT1], F32)) for i in range(2)]
            S6 = [E1(nc.sbuf_tensor(f"S6_{i}", [P, CC * TT1], F32)) for i in range(2)]
            zNb = [E1(nc.sbuf_tensor(f"zNb{i}", [P, CC * (TT1 + 1)], F32)) for i in range(2)]
            zDb = [E1(nc.sbuf_tensor(f"zDb{i}", [P, CC * (TT1 + 1)], F32)) for i in range(2)]
            mvb = [E1(nc.sbuf_tensor(f"mvb{i}", [P, 4 * TS1], F32)) for i in range(2)]
            stt_b = [E1(nc.sbuf_tensor(f"sttb{i}", [P, 12 * TS1], F32)) for i in range(2)]
            rstd = [E1(nc.sbuf_tensor(f"rstd{i}", [P, TS1], F32)) for i in range(2)]
            lnv = [E1(nc.sbuf_tensor(f"lnv{i}", [P, TS1], F32)) for i in range(2)]
            pp_in = [E1(nc.sbuf_tensor("pp_in0", [P, H], BF16)),
                     E1(nc.sbuf_tensor("pp_in1", [P, C], BF16))]

            psT = [E1(nc.psum_tensor(f"psT{i}", [P, C], BF16)) for i in range(2)]
            psP = [E1(nc.psum_tensor(f"psP{i}", [P, C], BF16)) for i in range(2)]
            psK = [E1(nc.psum_tensor(f"psK{i}", [P, 512], F32)) for i in range(2)]
            psR = [E1(nc.psum_tensor(f"psR{i}", [P, 512], F32)) for i in range(2)]

            r_wkT, r_wrT = Res("wkT"), Res("wrT")
            r_x_tok = [Res(f"x_tok{i}") for i in range(2)]
            r_xn_tok = [Res("xn_tok0")]
            r_xn_fm = [Res(f"xn_fm{i}") for i in range(2)]
            r_outb = [Res("outb0")]
            r_S1 = [[Res(f"S1_{i}_{c}") for c in range(CC)] for i in range(2)]
            r_S3 = [Res(f"S3_{i}") for i in range(2)]
            r_S4 = [[Res(f"S4_{i}_{c}") for c in range(CC)] for i in range(2)]
            r_S5, r_S6 = Res("S5"), Res("S6")
            r_s5 = [[Res(f"s5_{i}_{c}") for c in range(CC)] for i in range(2)]
            r_s6 = [[Res(f"s6_{i}_{c}") for c in range(CC)] for i in range(2)]
            r_zNb = [Res(f"zNb{i}") for i in range(2)]
            r_zDb = [Res(f"zDb{i}") for i in range(2)]
            r_ones = Res("onesb")
            r_mv = [Res(f"mv{i}") for i in range(2)]
            r_stt = [Res(f"stt{i}") for i in range(2)]
            r_rstd = [Res(f"rstd{i}") for i in range(2)]
            r_lnv = [Res(f"lnv{i}") for i in range(2)]
            r_pp_in = [Res(f"pp_in{i}") for i in range(2)]
            r_psT = [Res(f"psT{i}") for i in range(2)]
            r_psP = [Res(f"psP{i}") for i in range(2)]
            r_psK = [Res(f"psK{i}") for i in range(2)]
            r_psR = [Res(f"psR{i}") for i in range(2)]
            r_krtok = [Res(f"krtok{i}") for i in range(2)]

            bd.op("sync", lambda e: e.dma_start(out=ident[:], in_=id_d.ap()),
                  writes=[r_ident], dma_sem="d_setup")
            bd.op("sync", lambda e: e.dma_start(out=tdv[:], in_=tdp_d.ap()),
                  writes=[r_tdv], dma_sem="d_setup")
            bd.op("sync", lambda e: e.dma_start(out=tfv[:], in_=tfp_d.ap()),
                  writes=[r_tfv], dma_sem="d_setup")
            for i, v in [(0, EPS), (1, -0.5), (2, -1.0), (5, 127.0)]:
                bd.op("gpsimd", lambda e, i=i, v=v: e.memset(cst[:, i:i + 1], v),
                      writes=[r_cst])
            bd.op("vector", lambda e: e.reciprocal(cst[:, 3:4], cst[:, 5:6]),
                  reads=[r_cst], writes=[r_cst])
            bd.op("vector", lambda e: e.tensor_tensor(
                out=cst[:, 4:5], in0=cst[:, 3:4], in1=cst[:, 3:4], op=ALU.mult),
                reads=[r_cst], writes=[r_cst])
            bd.op("scalar", lambda e: e.activation(out=euv[:], in_=tfv[:], func=AF.Exp),
                  reads=[r_tfv], writes=[r_euv])
            bd.op("scalar", lambda e: e.activation(out=wneg[:], in_=tdv[:], func=AF.Exp),
                  reads=[r_tdv], writes=[r_wneg])
            bd.op("scalar", lambda e: e.activation(out=dv[:], in_=wneg[:], func=AF.Exp,
                                                   scale=cst[:, 2:3]),
                  reads=[r_wneg, r_cst], writes=[r_dv])
            zN3 = [zNb[i][:].rearrange("p (c t) -> p c t", t=TT1 + 1) for i in range(2)]
            zD3 = [zDb[i][:].rearrange("p (c t) -> p c t", t=TT1 + 1) for i in range(2)]
            bd.op("vector", lambda e: e.memset(zN3[1][:, :, TT1:TT1 + 1], 0.0), writes=[r_zNb[1]])
            bd.op("vector", lambda e: e.memset(zD3[1][:, :, TT1:TT1 + 1], 0.0), writes=[r_zDb[1]])
            bd.op("vector", lambda e: e.memset(zN3[0][:, :, 0:1], 0.0), writes=[r_zNb[0]])
            bd.op("vector", lambda e: e.memset(zD3[0][:, :, 0:1], 0.0), writes=[r_zDb[0]])

            def pp_mat_unit(src_d, row_cc, dstT, r_dstT, nrowchunks=CC):
                sl = pp_mat_unit.ctr % 2
                pp_mat_unit.ctr += 1
                src_ap = dram_ap(src_d, row_cc * P * C, [[C, P], [1, C]])

                def dma_part(sl=sl, src_ap=src_ap):
                    bd.op("gpsimd",
                          lambda e, sl=sl, src_ap=src_ap: e.dma_start(
                              out=pp_in[sl][:, 0:C], in_=src_ap),
                          writes=[r_pp_in[sl]], dma_sem=f"d_pp{sl}")

                def cmp_part(sl=sl, row_cc=row_cc):
                    def tr(e, sl=sl):
                        last = None
                        for cc in range(CC):
                            last = e.transpose(psP[sl][:, cc * P:(cc + 1) * P],
                                               pp_in[sl][:, cc * P:(cc + 1) * P], ident[:])
                        return last
                    bd.op("tensor", tr, reads=[r_pp_in[sl], r_ident], writes=[r_psP[sl]])
                    dst3 = dstT[:].rearrange("p (a b) -> p a b", b=C)
                    bd.op("scalar",
                          lambda e, sl=sl, row_cc=row_cc: e.copy(
                              dst3[:, :, row_cc * P:(row_cc + 1) * P], psP[sl][:, 0:C]),
                          reads=[r_psP[sl]], writes=[r_dstT])
                return dma_part, cmp_part, sl
            pp_mat_unit.ctr = 0

            def pp_wgu_unit(src_d, hc, iu):
                sl = pp_mat_unit.ctr % 2
                pp_mat_unit.ctr += 1
                ssl = pp_mat_unit.ctr % 2
                src_ap = dram_ap(src_d, hc * P * C, [[C, P], [1, C]])

                def dma_part(sl=sl, src_ap=src_ap):
                    bd.op("gpsimd",
                          lambda e, sl=sl, src_ap=src_ap: e.dma_start(
                              out=pp_in[sl][:, 0:C], in_=src_ap),
                          writes=[r_pp_in[sl]], dma_sem=f"d_pp{sl}")

                def cmp_part(sl=sl, ssl=ssl, hc=hc, iu=iu):
                    def tr(e, sl=sl):
                        last = None
                        for cc in range(CC):
                            last = e.transpose(psP[sl][:, cc * P:(cc + 1) * P],
                                               pp_in[sl][:, cc * P:(cc + 1) * P], ident[:])
                        return last
                    bd.op("tensor", tr, reads=[r_pp_in[sl], r_ident], writes=[r_psP[sl]])
                    bd.op("scalar",
                          lambda e, sl=sl, ssl=ssl: e.copy(pp_stage[ssl][:], psP[sl][:, 0:C]),
                          reads=[r_psP[sl]], writes=[r_pp_stage[ssl]])
                    dst = dram_ap(wgu_s, ((hc * 2 + iu) * CC) * P * P,
                                  [[P, P], [P * P, CC], [1, P]])
                    bd.op("sync",
                          lambda e, ssl=ssl, dst=dst: e.dma_start(out=dst, in_=pp_stage[ssl][:]),
                          reads=[r_pp_stage[ssl]], writes=[r_wgu_s[hc]], dma_sem=f"d_pps{ssl}")
                return dma_part, cmp_part, sl

            def pp_wd_unit(dc):
                sl = 0
                pp_mat_unit.ctr += 1
                ctr = pp_mat_unit.ctr
                src_ap = dram_ap(wd_d, dc * P * H, [[H, P], [1, H]])

                def dma_part(sl=sl, src_ap=src_ap):
                    bd.op("gpsimd",
                          lambda e, sl=sl, src_ap=src_ap: e.dma_start(
                              out=pp_in[sl][:, 0:H], in_=src_ap),
                          writes=[r_pp_in[sl]], dma_sem=f"d_pp{sl}")

                def cmp_part(sl=sl, dc=dc, ctr=ctr):
                    for g in range(3):
                        psl = (sl + g) % 2
                        def tr(e, sl=sl, g=g, psl=psl):
                            last = None
                            for j in range(CC):
                                hc = g * CC + j
                                last = e.transpose(psP[psl][:, j * P:(j + 1) * P],
                                                   pp_in[sl][:, hc * P:(hc + 1) * P], ident[:])
                            return last
                        bd.op("tensor", tr, reads=[r_pp_in[sl], r_ident], writes=[r_psP[psl]])
                        ssl = (ctr + g) % 2
                        bd.op("scalar",
                              lambda e, psl=psl, ssl=ssl: e.copy(pp_stage[ssl][:], psP[psl][:, 0:C]),
                              reads=[r_psP[psl]], writes=[r_pp_stage[ssl]])
                        dst = dram_ap(wdT_s, (g * CC) * P * C + dc * P,
                                      [[C, P], [P * C, CC], [1, P]])
                        bd.op("sync",
                              lambda e, ssl=ssl, dst=dst: e.dma_start(out=dst, in_=pp_stage[ssl][:]),
                              reads=[r_pp_stage[ssl]],
                              writes=[r_wdT_s[g * CC + j] for j in range(CC)],
                              dma_sem=f"d_pps{ssl}")
                return dma_part, cmp_part, 0

            pp_stage = [E1(nc.sbuf_tensor(f"pp_stage{i}", [P, C], BF16)) for i in range(2)]
            r_pp_stage = [Res(f"pp_stage{i}") for i in range(2)]


            startup = [pp_mat_unit(wk_d, cc, wkT, r_wkT) for cc in range(CC)]
            startup += [pp_mat_unit(wr_d, cc, wrT, r_wrT) for cc in range(CC)]
            startup[0][0]()
            startup[1][0]()
            for i, u in enumerate(startup):
                u[1]()
                if i + 2 < len(startup):
                    startup[i + 2][0]()

            pp_units = []
            for cc in range(CC):
                pp_units.append(lambda cc=cc: pp_mat_unit(wo_d, cc, woT, r_woT))
            for hc in range(HC):
                pp_units.append(lambda hc=hc: pp_wgu_unit(wg_d, hc, 0))
                pp_units.append(lambda hc=hc: pp_wgu_unit(wu_d, hc, 1))
            for dc in range(CC):
                pp_units.append(lambda dc=dc: pp_wd_unit(dc))
            pp_idx = 0
            pp_dma_idx = 0
            pp_pairs = []

            def pp_get(i):
                while len(pp_pairs) <= i:
                    pp_pairs.append(pp_units[len(pp_pairs)]())
                return pp_pairs[i]

            def pp_can_dma(j, cmp_done):
                sj = pp_get(j)[2]
                for i in range(j - 1, -1, -1):
                    if pp_get(i)[2] == sj:
                        return i < cmp_done
                return True

            def pp_drip(n_units):
                nonlocal pp_idx, pp_dma_idx
                while pp_idx < n_units:
                    while pp_dma_idx <= pp_idx:
                        pp_get(pp_dma_idx)[0]()
                        pp_dma_idx += 1
                    pp_get(pp_idx)[1]()
                    pp_idx += 1
                    while (pp_dma_idx < min(pp_idx + 2, len(pp_units))
                           and pp_can_dma(pp_dma_idx, pp_idx)):
                        pp_get(pp_dma_idx)[0]()
                        pp_dma_idx += 1

            def stage_a_dma(ch):
                t0 = ch * TT1
                sl = ch % 2
                src = dram_ap(x_d, t0 * C, [[C, P], [P * C, TS1], [1, C]])
                bd.op("sync",
                      lambda e, sl=sl, src=src: e.dma_start(out=x_tok[sl][:], in_=src),
                      writes=[r_x_tok[sl]], dma_sem=f"d_x{sl}")

            def stage_a_stats(ch):
                sl = ch % 2
                for ts in range(TS1):
                    for hh in range(2):
                        bd.op("vector",
                              lambda e, sl=sl, ts=ts, hh=hh: e.bn_stats(
                                  stt_b[sl][:, (ts * 2 + hh) * 6:(ts * 2 + hh + 1) * 6],
                                  x_tok[sl][:, ts * C + hh * 512: ts * C + (hh + 1) * 512]),
                              reads=[r_x_tok[sl]], writes=[r_stt[sl]])
                    bd.op("vector",
                          lambda e, sl=sl, ts=ts: e.bn_aggr(
                              mvb[sl][:, ts * 2:(ts + 1) * 2],
                              stt_b[sl][:, ts * 12:(ts * 12) + 12]),
                          reads=[r_stt[sl]], writes=[r_mv[sl]])
                    bd.op("scalar",
                          lambda e, sl=sl, ts=ts: e.activation(
                              out=lnv[sl][:, ts:ts + 1], in_=mvb[sl][:, ts * 2 + 1:ts * 2 + 2],
                              func=AF.Ln, bias=cst[:, 0:1]),
                          reads=[r_mv[sl], r_cst], writes=[r_lnv[sl]])
                    bd.op("scalar",
                          lambda e, sl=sl, ts=ts: e.activation(
                              out=rstd[sl][:, ts:ts + 1], in_=lnv[sl][:, ts:ts + 1],
                              func=AF.Exp, scale=cst[:, 1:2]),
                          reads=[r_lnv[sl], r_cst], writes=[r_rstd[sl]])

            def stage_a_apply(ch):
                sl = ch % 2
                for ts in range(TS1):
                    bd.op("vector",
                          lambda e, sl=sl, ts=ts: e.tensor_scalar(
                              out=xn_tok[0][:, ts * C:(ts + 1) * C],
                              in0=x_tok[sl][:, ts * C:(ts + 1) * C],
                              scalar1=mvb[sl][:, ts * 2:ts * 2 + 1], op0=ALU.subtract,
                              scalar2=rstd[sl][:, ts:ts + 1], op1=ALU.mult),
                          reads=[r_x_tok[sl], r_mv[sl], r_rstd[sl]], writes=[r_xn_tok[0]])

            def stage_a_post(ch):
                t0 = ch * TT1
                sl = ch % 2
                for ts in range(TS1):
                    def tr(e, sl=sl, ts=ts):
                        last = None
                        for cc in range(CC):
                            last = e.transpose(psT[ts][:, cc * P:(cc + 1) * P],
                                               xn_tok[0][:, ts * C + cc * P: ts * C + (cc + 1) * P],
                                               ident[:])
                        return last
                    bd.op("tensor", tr, reads=[r_xn_tok[0], r_ident], writes=[r_psT[ts]])
                    xnf3 = xn_fm[sl][:].rearrange("p (a b) -> p a b", b=TT1)
                    bd.op("scalar",
                          lambda e, sl=sl, ts=ts, xnf3=xnf3: e.copy(
                              xnf3[:, :, ts * P:(ts + 1) * P], psT[ts][:, 0:C]),
                          reads=[r_psT[ts]], writes=[r_xn_fm[sl]])
                dst = dram_ap(xn_s, t0, [[T, P], [P * T, CC], [1, TT1]])
                bd.op("sync", lambda e, sl=sl, dst=dst: e.dma_start(out=dst, in_=xn_fm[sl][:]),
                      reads=[r_xn_fm[sl]], writes=[r_xn_s[ch]], dma_sem=f"d_xns{sl}")

            def stage_b(ch):
                sl = ch % 2
                for pr in range(CC // 2):
                    pk = pr % 2
                    for half in range(2):
                        cc_o = pr * 2 + half
                        for (wT, r_wT, psx, r_psx) in ((wkT, r_wkT, psK, r_psK),
                                                       (wrT, r_wrT, psR, r_psR)):
                            def mm(e, sl=sl, cc_o=cc_o, wT=wT, psx=psx, pk=pk, half=half):
                                last = None
                                for cc_i in range(CC):
                                    last = e.matmul(
                                        psx[pk][:, half * TT1:(half + 1) * TT1],
                                        wT[:, cc_i * C + cc_o * P: cc_i * C + (cc_o + 1) * P],
                                        xn_fm[sl][:, cc_i * TT1:(cc_i + 1) * TT1],
                                        start=(cc_i == 0), stop=(cc_i == CC - 1),
                                        skip_group_check=True)
                                return last
                            bd.op("tensor", mm, reads=[r_wT, r_xn_fm[sl]], writes=[r_psx[pk]])
                    c0 = pr * 2
                    bd.op("scalar",
                          lambda e, sl=sl, c0=c0, pk=pk: e.activation(
                              out=S1[sl][:, c0 * TT1:(c0 + 2) * TT1],
                              in_=psK[pk][:], func=AF.Exp),
                          reads=[r_psK[pk]], writes=[r_S1[sl][c0], r_S1[sl][c0 + 1]])
                    bd.op("scalar",
                          lambda e, sl=sl, c0=c0, pk=pk: e.activation(
                              out=S3[sl][:, c0 * TT1:(c0 + 2) * TT1],
                              in_=psR[pk][:], func=AF.Exp, scale=cst[:, 2:3]),
                          reads=[r_psR[pk], r_cst], writes=[r_S3[sl], r_krtok[pk]])
                    bd.op("vector",
                          lambda e, sl=sl, c0=c0, pk=pk: e.tensor_tensor(
                              out=S4[sl][:, c0 * TT1:(c0 + 2) * TT1],
                              in0=S1[sl][:, c0 * TT1:(c0 + 2) * TT1],
                              in1=psR[pk][:], op=ALU.mult),
                          reads=[r_S1[sl][c0], r_S1[sl][c0 + 1], r_psR[pk],
                                 r_krtok[pk]],
                          writes=[r_S4[sl][c0], r_S4[sl][c0 + 1]])
                if ch > 0:
                    ol = 1 - sl
                    bd.op("scalar",
                          lambda e, sl=sl, ol=ol: e.copy(zN3[sl][:, :, 0:1],
                                                         zN3[ol][:, :, TT1:TT1 + 1]),
                          reads=[r_zNb[ol]], writes=[r_zNb[sl]])
                    bd.op("scalar",
                          lambda e, sl=sl, ol=ol: e.copy(zD3[sl][:, :, 0:1],
                                                         zD3[ol][:, :, TT1:TT1 + 1]),
                          reads=[r_zDb[ol]], writes=[r_zDb[sl]])

            def stage_c_scans(ch, lo, hi):
                sl = ch % 2
                ol = 1 - sl
                for cc in range(lo, hi):
                    b0 = cc * (TT1 + 1)
                    bd.op("vector",
                          lambda e, sl=sl, ol=ol, cc=cc, b0=b0: e.tensor_tensor_scan(
                              out=zDb[sl][:, b0 + 1: b0 + 1 + TT1],
                              data0=bass.AP(dv, cc, [[CC, P], [0, TT1]]),
                              data1=S1[sl][:, cc * TT1:(cc + 1) * TT1],
                              initial=zDb[ol][:, b0 + TT1: b0 + TT1 + 1],
                              op0=ALU.mult, op1=ALU.add),
                          reads=[r_dv, r_S1[sl][cc], r_zDb[ol]], writes=[r_zDb[sl]])
                    bd.op("vector",
                          lambda e, sl=sl, ol=ol, cc=cc, b0=b0: e.tensor_tensor_scan(
                              out=zNb[sl][:, b0 + 1: b0 + 1 + TT1],
                              data0=bass.AP(dv, cc, [[CC, P], [0, TT1]]),
                              data1=S4[sl][:, cc * TT1:(cc + 1) * TT1],
                              initial=zNb[ol][:, b0 + TT1: b0 + TT1 + 1],
                              op0=ALU.mult, op1=ALU.add),
                          reads=[r_dv, r_S4[sl][cc], r_zNb[ol]], writes=[r_zNb[sl]])

            def stage_c_stt(ch):
                sl = ch % 2
                for cc in range(CC):
                    sli = slice(cc * TT1, (cc + 1) * TT1)
                    bd.op("vector",
                          lambda e, sl=sl, cc=cc, sli=sli: e.scalar_tensor_tensor(
                              out=S5[sl][:, sli], in0=S1[sl][:, sli],
                              scalar=euv[:, cc:cc + 1],
                              in1=zD3[sl][:, cc, 0:TT1], op0=ALU.mult, op1=ALU.add),
                          reads=[r_S1[sl][cc], r_euv, r_zDb[sl]], writes=[r_s5[sl][cc]])
                    bd.op("vector",
                          lambda e, sl=sl, sli=sli, cc=cc: e.scalar_tensor_tensor(
                              out=S6[sl][:, sli], in0=S3[sl][:, sli], scalar=1.0,
                              in1=S5[sl][:, sli], op0=ALU.add, op1=ALU.mult),
                          reads=[r_S3[sl], r_s5[sl][cc]], writes=[r_s6[sl][cc]])
            def stage_d(ch):
                sl = ch % 2
                for cc in range(CC):
                    sli = slice(cc * TT1, (cc + 1) * TT1)
                    bd.op("scalar",
                          lambda e, sl=sl, sli=sli: e.activation(
                              out=S5[sl][:, sli], in_=S6[sl][:, sli], func=AF.Ln),
                          reads=[r_s6[sl][cc]], writes=[r_s5[sl][cc]])
                    bd.op("scalar",
                          lambda e, sl=sl, sli=sli: e.activation(
                              out=S6[sl][:, sli], in_=S5[sl][:, sli],
                              func=AF.Exp, scale=cst[:, 2:3]),
                          reads=[r_s5[sl][cc], r_cst], writes=[r_s6[sl][cc]])

            def stage_e(ch):
                t0 = ch * TT1
                sl = ch % 2
                for cc in range(CC):
                    bd.op("vector",
                          lambda e, sl=sl, cc=cc: e.scalar_tensor_tensor(
                              out=S5[sl][:, cc * TT1:(cc + 1) * TT1],
                              in0=S4[sl][:, cc * TT1:(cc + 1) * TT1],
                              scalar=euv[:, cc:cc + 1],
                              in1=zN3[sl][:, cc, 0:TT1], op0=ALU.mult, op1=ALU.add),
                          reads=[r_S4[sl][cc], r_euv, r_zNb[sl]],
                          writes=[r_s5[sl][cc]])
                bd.op("gpsimd", lambda e, sl=sl: e.tensor_tensor(
                    out=outb[0][:], in0=S5[sl][:], in1=S6[sl][:], op=ALU.mult),
                    reads=[r_s5[sl][cc] for cc in range(CC)]
                          + [r_s6[sl][cc] for cc in range(CC)],
                    writes=[r_outb[0]])
                dst = dram_ap(outp_s, t0, [[T, P], [P * T, CC], [1, TT1]])
                bd.op("sync", lambda e, dst=dst: e.dma_start(out=dst, in_=outb[0][:]),
                      reads=[r_outb[0]], writes=[r_out_s[ch]], dma_sem="d_ots0")

            stage_a_dma(0)
            stage_a_dma(1)
            stage_a_stats(0)
            stage_a_apply(0)
            stage_a_post(0)
            stage_b(0)
            stage_a_dma(2)
            stage_a_stats(1)
            stage_a_apply(1)
            stage_a_post(1)
            while (pp_dma_idx < min(2, len(pp_units))
                   and pp_can_dma(pp_dma_idx, pp_idx)):
                pp_get(pp_dma_idx)[0]()
                pp_dma_idx += 1
            for ch in range(NC1):
                if ch + 3 < NC1:
                    stage_a_dma(ch + 3)
                if ch + 2 < NC1:
                    stage_a_stats(ch + 2)
                    stage_a_apply(ch + 2)
                stage_c_scans(ch, 0, CC)
                stage_c_stt(ch)
                if ch > 0:
                    stage_e(ch - 1)
                if ch + 1 < NC1:
                    stage_b(ch + 1)
                if ch + 2 < NC1:
                    stage_a_post(ch + 2)
                pp_drip((len(pp_units) * (ch + 1)) // NC1)
                stage_d(ch)
            stage_e(NC1 - 1)

            pp_drip(len(pp_units))

            bd.finalize(block)

        with contextlib.ExitStack() as ph2:
            E2 = ph2.enter_context
            xnc = [E2(nc.sbuf_tensor(f"xnc{i}", [P, CC * TT2], BF16)) for i in range(2)]
            otc = [E2(nc.sbuf_tensor(f"otc{i}", [P, CC * TT2], BF16)) for i in range(2)]
            x2 = [E2(nc.sbuf_tensor(f"x2_{i}", [P, TS2 * C], F32)) for i in range(2)]
            res_t = [E2(nc.sbuf_tensor(f"res{i}", [P, TS2 * C], F32)) for i in range(2)]
            h_sb = E2(nc.sbuf_tensor("h_sb", [P, HC * TT2], BF16))
            wgu_t = [E2(nc.sbuf_tensor(f"wgu{i}", [P, 2 * CC * P], BF16)) for i in range(4)]
            wd_t = [E2(nc.sbuf_tensor(f"wdt{i}", [P, C], BF16)) for i in range(4)]
            sil_b = [E2(nc.sbuf_tensor(f"sil{i}", [P, TT2], F32)) for i in range(2)]

            psA = [E2(nc.psum_tensor(f"psA{i}", [P, TT2], F32)) for i in range(2)]
            psB = [E2(nc.psum_tensor(f"psB{i}", [P, TT2], F32)) for i in range(2)]
            psRb = [E2(nc.psum_tensor(f"psRb{i}", [P, TT2], F32)) for i in range(4)]

            r_xnc = [Res(f"xnc{i}") for i in range(2)]
            r_otc = [Res(f"otc{i}") for i in range(2)]
            r_x2 = [Res(f"x2_{i}") for i in range(2)]
            r_res = [Res(f"res{i}") for i in range(2)]
            r_h = Res("h_sb")
            r_wgu = [Res(f"wgu{i}") for i in range(4)]
            r_wd = [Res(f"wdt{i}") for i in range(4)]
            r_sil = [Res(f"sil{i}") for i in range(2)]
            r_psA = [Res(f"psA{i}") for i in range(2)]
            r_psB = [Res(f"psB{i}") for i in range(2)]
            r_psRb = [Res(f"psRb{i}") for i in range(4)]
            banks = [r_psRb[0], r_psRb[1], r_psRb[2], r_psRb[3],
                     r_psA[0], r_psA[1], r_psB[0], r_psB[1]]
            bank_t = [psRb[0], psRb[1], psRb[2], psRb[3], psA[0], psA[1], psB[0], psB[1]]

            bd.fence()

            for ch in range(NC2):
                t0 = ch * TT2
                sl = ch % 2
                src = dram_ap(xn_s, t0, [[T, P], [P * T, CC], [1, TT2]])
                bd.op("sync", lambda e, sl=sl, src=src: e.dma_start(out=xnc[sl][:], in_=src),
                      reads=[r_xn_s[2 * ch], r_xn_s[2 * ch + 1]],
                      writes=[r_xnc[sl]], dma_sem=f"d_xnc{sl}")
                src = dram_ap(outp_s, t0, [[T, P], [P * T, CC], [1, TT2]])
                bd.op("sync", lambda e, sl=sl, src=src: e.dma_start(out=otc[sl][:], in_=src),
                      reads=[r_out_s[2 * ch], r_out_s[2 * ch + 1]],
                      writes=[r_otc[sl]], dma_sem=f"d_otc{sl}")
                src = dram_ap(x_d, t0 * C, [[C, P], [P * C, TS2], [1, C]])
                bd.op("sync", lambda e, sl=sl, src=src: e.dma_start(out=x2[sl][:], in_=src),
                      writes=[r_x2[sl]], dma_sem=f"d_x2{sl}")

                for hc in range(HC):
                    wsl = hc % 4
                    src = dram_ap(wgu_s, hc * 2 * CC * P * P,
                                  [[P, P], [CC * P * P, 2], [P * P, CC], [1, P]])
                    bd.op("sync",
                          lambda e, wsl=wsl, src=src: e.dma_start(out=wgu_t[wsl][:], in_=src),
                          reads=[r_wgu_s[hc]], writes=[r_wgu[wsl]], dma_sem=f"d_wgu{wsl}")

                    pab = hc % 2
                    def mmg(e, sl=sl, wsl=wsl, pab=pab, iu=0):
                        last = None
                        for cc in range(CC):
                            last = e.matmul(
                                psA[pab][:],
                                wgu_t[wsl][:, (iu * CC + cc) * P:(iu * CC + cc + 1) * P],
                                xnc[sl][:, cc * TT2:(cc + 1) * TT2],
                                start=(cc == 0), stop=(cc == CC - 1))
                        return last
                    bd.op("tensor", mmg, reads=[r_wgu[wsl], r_xnc[sl]], writes=[r_psA[pab]])

                    def mmu(e, sl=sl, wsl=wsl, pab=pab, iu=1):
                        last = None
                        for cc in range(CC):
                            last = e.matmul(
                                psB[pab][:],
                                wgu_t[wsl][:, (iu * CC + cc) * P:(iu * CC + cc + 1) * P],
                                xnc[sl][:, cc * TT2:(cc + 1) * TT2],
                                start=(cc == 0), stop=(cc == CC - 1))
                        return last
                    bd.op("tensor", mmu, reads=[r_wgu[wsl], r_xnc[sl]], writes=[r_psB[pab]])
                    bd.op("scalar",
                          lambda e, pab=pab: e.activation(
                              out=sil_b[pab][:], in_=psA[pab][:], func=AF.Silu,
                              scale=cst[:, 3:4]),
                          reads=[r_psA[pab], r_cst], writes=[r_sil[pab]])
                    bd.op("vector",
                          lambda e, hc=hc, pab=pab: e.scalar_tensor_tensor(
                              out=h_sb[:, hc * TT2:(hc + 1) * TT2],
                              in0=sil_b[pab][:], scalar=cst[:, 4:5], in1=psB[pab][:],
                              op0=ALU.mult, op1=ALU.mult),
                          reads=[r_sil[pab], r_psB[pab]], writes=[r_h])

                for hc in range(HC):
                    wsl = hc % 4
                    src = dram_ap(wdT_s, hc * P * C, [[C, P], [1, C]])
                    bd.op("sync",
                          lambda e, wsl=wsl, src=src: e.dma_start(out=wd_t[wsl][:], in_=src),
                          reads=[r_wdT_s[hc]], writes=[r_wd[wsl]], dma_sem=f"d_wd{wsl}")

                    def mmd(e, hc=hc, wsl=wsl):
                        last = None
                        for ts in range(TS2):
                            for dh in range(2):
                                last = e.matmul(
                                    bank_t[ts * 2 + dh][:],
                                    h_sb[:, hc * TT2 + ts * P: hc * TT2 + (ts + 1) * P],
                                    wd_t[wsl][:, dh * 512:(dh + 1) * 512],
                                    start=(hc == 0), stop=False, skip_group_check=True)
                        return last
                    bd.op("tensor", mmd, reads=[r_h, r_wd[wsl]], writes=banks)
                for hc2 in range(CC):
                    def mmw(e, sl=sl, hc2=hc2):
                        last = None
                        for ts in range(TS2):
                            for dh in range(2):
                                last = e.matmul(
                                    bank_t[ts * 2 + dh][:],
                                    otc[sl][:, hc2 * TT2 + ts * P: hc2 * TT2 + (ts + 1) * P],
                                    woT[:, hc2 * C + dh * 512: hc2 * C + (dh + 1) * 512],
                                    start=False, stop=(hc2 == CC - 1), skip_group_check=True)
                        return last
                    bd.op("tensor", mmw, reads=[r_otc[sl], r_woT], writes=banks)
                for grp in range(2 * TS2):
                    bd.op("vector",
                          lambda e, sl=sl, grp=grp: e.tensor_tensor(
                              out=res_t[sl][:, grp * 512:(grp + 1) * 512],
                              in0=bank_t[grp][:],
                              in1=x2[sl][:, grp * 512:(grp + 1) * 512], op=ALU.add),
                          reads=[banks[grp], r_x2[sl]], writes=[r_res[sl]])
                dst = dram_ap(out_d, t0 * C, [[C, P], [P * C, TS2], [1, C]])
                bd.op("gpsimd", lambda e, sl=sl, dst=dst: e.dma_start(out=dst, in_=res_t[sl][:]),
                      reads=[r_res[sl]], dma_sem=f"d_rs{sl}")

            bd.fence(("sync", "gpsimd"))
            bd.finalize(block)

    return nc


_PROG_CACHE = {}


def kernel(L0_attn_x, L0_attn_Wk, L0_attn_Wr, L0_attn_Wo,
           L0_attn_time_decay, L0_attn_time_first,
           L0_ffn_w_gate, L0_ffn_w_up, L0_ffn_w_down):
    x = np.ascontiguousarray(np.asarray(L0_attn_x, dtype=np.float32))
    B, T, C_ = x.shape
    wk = np.ascontiguousarray(np.asarray(L0_attn_Wk, dtype=np.float32))
    wr = np.ascontiguousarray(np.asarray(L0_attn_Wr, dtype=np.float32))
    wo = np.ascontiguousarray(np.asarray(L0_attn_Wo, dtype=np.float32))
    wg = np.ascontiguousarray(np.asarray(L0_ffn_w_gate, dtype=np.int32))
    wu = np.ascontiguousarray(np.asarray(L0_ffn_w_up, dtype=np.int32))
    wd = np.ascontiguousarray(np.asarray(L0_ffn_w_down, dtype=np.int32))
    td = np.asarray(L0_attn_time_decay, dtype=np.float32)
    tf = np.asarray(L0_attn_time_first, dtype=np.float32)
    tdp = np.ascontiguousarray(td.reshape(CC, P).T)
    tfp = np.ascontiguousarray(tf.reshape(CC, P).T)
    ident = np.eye(P, dtype=ml_dtypes.bfloat16)

    if T not in _PROG_CACHE:
        _PROG_CACHE[T] = build_program(T)
    nc = _PROG_CACHE[T]

    in_maps = []
    for b in range(B):
        in_maps.append(dict(x=np.ascontiguousarray(x[b]), Wk=wk, Wr=wr, Wo=wo,
                            wg=wg, wu=wu, wd=wd, tdp=tdp, tfp=tfp, ident=ident))
    res = run_bass_kernel_spmd(nc, in_maps, core_ids=list(range(B)))
    out = np.stack([np.asarray(r["out"], dtype=np.float32) for r in res.results], axis=0)
    return out


# version 32
# speedup vs baseline: 1.0867x; 1.0867x over previous
import contextlib
import numpy as np
import ml_dtypes
import concourse.bass as bass
import concourse.mybir as mybir
from concourse.bass_utils import run_bass_kernel_spmd

F32 = mybir.dt.float32
BF16 = mybir.dt.bfloat16
I32 = mybir.dt.int32
AF = mybir.ActivationFunctionType
ALU = mybir.AluOpType

P = 128
C = 1024
H = 3072
CC = C // P
HC = H // P
EPS = 1e-5
INV127 = 1.0 / 127.0
INV127SQ = INV127 * INV127

ENGINES = ("tensor", "vector", "scalar", "gpsimd", "sync")
ENG_SEM = {"tensor": "pe", "vector": "dve", "scalar": "act", "gpsimd": "pool"}


class Res:
    __slots__ = ("writer", "readers", "name")

    def __init__(self, name):
        self.writer = None
        self.readers = []
        self.name = name


class Build:

    def __init__(self, nc):
        self.nc = nc
        self.q = {e: [] for e in ENGINES}
        self.sems = {}
        self.semv = {}
        self.waited = {e: {} for e in ENGINES}

    def add_sem(self, name, handle):
        self.sems[name] = handle
        self.semv[name] = 0

    def op(self, eng, fn, reads=(), writes=(), dma_sem=None):
        edges = {}

        def add(t):
            if t is not None:
                s, v = t
                if edges.get(s, 0) < v:
                    edges[s] = v

        for r in reads:
            add(r.writer)
        for w in writes:
            add(w.writer)
            for t in w.readers:
                add(t)
        if dma_sem is not None:
            add((dma_sem, self.semv[dma_sem]))
        wl = []
        for s, v in sorted(edges.items()):
            if self.waited[eng].get(s, 0) >= v:
                continue
            wl.append((s, v))
            self.waited[eng][s] = v
        sem = dma_sem if dma_sem is not None else ENG_SEM[eng]
        inc = 16 if dma_sem is not None else 1
        self.semv[sem] += inc
        tick = (sem, self.semv[sem])

        def emit(e, wl=wl, fn=fn, sem=sem, inc=inc):
            for s, v in wl:
                e.wait_ge(self.sems[s], v)
            inst = fn(e)
            inst.then_inc(self.sems[sem], inc)

        self.q[eng].append(emit)
        for w in writes:
            w.writer = tick
            w.readers = []
        for r in reads:
            r.readers.append(tick)
        return tick

    def raw_wait(self, eng, sem, val):
        def emit(e, sem=sem, val=val):
            e.wait_ge(self.sems[sem], val)
        self.q[eng].append(emit)

    def fence(self, engs=ENGINES):
        for eng in engs:
            for s, v in list(self.semv.items()):
                if v == 0:
                    continue
                if self.waited[eng].get(s, 0) >= v:
                    continue
                self.raw_wait(eng, s, v)
                self.waited[eng][s] = v

    def finalize(self, block):
        for eng in ENGINES:
            ops = self.q[eng]
            if not ops:
                continue
            section = getattr(block, eng)

            def body(e, ops=ops):
                for f in ops:
                    f(e)
            section(body)
            self.q[eng] = []


def dram_ap(handle, offset, dims):
    return bass.AP(handle, offset, [list(d) for d in dims])


def build_program(T=4096):
    TT1 = 256
    NC1 = T // TT1
    TS1 = TT1 // P
    TT2 = 512
    NC2 = T // TT2
    TS2 = TT2 // P

    nc = bass.Bass("TRN2", target_bir_lowering=False)

    x_d = nc.dram_tensor("x", [T, C], F32, kind="ExternalInput")
    wk_d = nc.dram_tensor("Wk", [C, C], F32, kind="ExternalInput")
    wr_d = nc.dram_tensor("Wr", [C, C], F32, kind="ExternalInput")
    wo_d = nc.dram_tensor("Wo", [C, C], F32, kind="ExternalInput")
    wg_d = nc.dram_tensor("wg", [H, C], I32, kind="ExternalInput")
    wu_d = nc.dram_tensor("wu", [H, C], I32, kind="ExternalInput")
    wd_d = nc.dram_tensor("wd", [C, H], I32, kind="ExternalInput")
    tdp_d = nc.dram_tensor("tdp", [P, CC], F32, kind="ExternalInput")
    tfp_d = nc.dram_tensor("tfp", [P, CC], F32, kind="ExternalInput")
    id_d = nc.dram_tensor("ident", [P, P], BF16, kind="ExternalInput")
    out_d = nc.dram_tensor("out", [T, C], F32, kind="ExternalOutput")

    wgu_s = nc.dram_tensor("wgu_s", [HC, 2, CC, P, P], BF16, kind="Internal")
    wdT_s = nc.dram_tensor("wdT_s", [HC, P, C], BF16, kind="Internal")
    xn_s = nc.dram_tensor("xn_s", [CC, P, T], BF16, kind="Internal")
    outp_s = nc.dram_tensor("outp_s", [CC, P, T], BF16, kind="Internal")

    bd = Build(nc)

    with contextlib.ExitStack() as top:
        E = top.enter_context
        for s in ("pe", "dve", "act", "pool"):
            bd.add_sem(s, E(nc.semaphore(s)))
        dma_sems = [
            "d_setup",
            "d_x0", "d_x1",
            "d_xns0", "d_xns1",
            "d_ots0", "d_ots1",
            "d_pp0", "d_pp1",
            "d_pps0", "d_pps1",
            "d_xnc0", "d_xnc1",
            "d_otc0", "d_otc1",
            "d_x20", "d_x21",
            "d_wgu0", "d_wgu1", "d_wgu2", "d_wgu3",
            "d_wd0", "d_wd1", "d_wd2", "d_wd3",
            "d_rs0", "d_rs1",
        ]
        for s in dma_sems:
            bd.add_sem(s, E(nc.semaphore(s)))

        block = E(nc.Block())

        ident = E(nc.sbuf_tensor("identsb", [P, P], BF16))
        woT = E(nc.sbuf_tensor("woT", [P, CC * C], BF16))
        cst = E(nc.sbuf_tensor("cst", [P, 8], F32))
        euv = E(nc.sbuf_tensor("euv", [P, CC], F32))
        dv = E(nc.sbuf_tensor("dv", [P, CC], F32))
        tdv = E(nc.sbuf_tensor("tdv", [P, CC], F32))
        tfv = E(nc.sbuf_tensor("tfv", [P, CC], F32))
        wneg = E(nc.sbuf_tensor("wneg", [P, CC], F32))

        r_ident = Res("ident")
        r_woT = Res("woT")
        r_cst = Res("cst")
        r_euv = Res("euv")
        r_dv = Res("dv")
        r_tdv = Res("tdv")
        r_tfv = Res("tfv")
        r_wneg = Res("wneg")
        r_wgu_s = [Res(f"wgu_s{hc}") for hc in range(HC)]
        r_wdT_s = [Res(f"wdT_s{hc}") for hc in range(HC)]
        r_xn_s = [Res(f"xn_s{ch}") for ch in range(NC1)]
        r_out_s = [Res(f"out_s{ch}") for ch in range(NC1)]

        with contextlib.ExitStack() as ph1:
            E1 = ph1.enter_context
            wkT = E1(nc.sbuf_tensor("wkT", [P, CC * C], BF16))
            wrT = E1(nc.sbuf_tensor("wrT", [P, CC * C], BF16))
            x_tok = [E1(nc.sbuf_tensor(f"x_tok{i}", [P, TS1 * C], F32)) for i in range(2)]
            xn_tok = [E1(nc.sbuf_tensor("xn_tok0", [P, TS1 * C], BF16))]
            xn_fm = [E1(nc.sbuf_tensor(f"xn_fm{i}", [P, CC * TT1], BF16)) for i in range(2)]
            outb = [E1(nc.sbuf_tensor("outb0", [P, CC * TT1], BF16))]
            S1 = [E1(nc.sbuf_tensor(f"S1_{i}", [P, CC * TT1], F32)) for i in range(2)]
            S3 = [E1(nc.sbuf_tensor(f"S3_{i}", [P, CC * TT1], F32)) for i in range(2)]
            S4 = [E1(nc.sbuf_tensor(f"S4_{i}", [P, CC * TT1], F32)) for i in range(2)]
            S5 = [E1(nc.sbuf_tensor(f"S5_{i}", [P, CC * TT1], F32)) for i in range(2)]
            S6 = [E1(nc.sbuf_tensor(f"S6_{i}", [P, CC * TT1], F32)) for i in range(2)]
            zNb = [E1(nc.sbuf_tensor(f"zNb{i}", [P, CC * (TT1 + 1)], F32)) for i in range(2)]
            zDb = [E1(nc.sbuf_tensor(f"zDb{i}", [P, CC * (TT1 + 1)], F32)) for i in range(2)]
            mvb = [E1(nc.sbuf_tensor(f"mvb{i}", [P, 4 * TS1], F32)) for i in range(2)]
            stt_b = [E1(nc.sbuf_tensor(f"sttb{i}", [P, 12 * TS1], F32)) for i in range(2)]
            rstd = [E1(nc.sbuf_tensor(f"rstd{i}", [P, TS1], F32)) for i in range(2)]
            lnv = [E1(nc.sbuf_tensor(f"lnv{i}", [P, TS1], F32)) for i in range(2)]
            pp_in = [E1(nc.sbuf_tensor("pp_in0", [P, H], BF16)),
                     E1(nc.sbuf_tensor("pp_in1", [P, C], BF16))]

            psT = [E1(nc.psum_tensor(f"psT{i}", [P, C], BF16)) for i in range(2)]
            psP = [E1(nc.psum_tensor(f"psP{i}", [P, C], BF16)) for i in range(2)]
            psK = [E1(nc.psum_tensor(f"psK{i}", [P, 512], F32)) for i in range(2)]
            psR = [E1(nc.psum_tensor(f"psR{i}", [P, 512], F32)) for i in range(2)]

            r_wkT, r_wrT = Res("wkT"), Res("wrT")
            r_x_tok = [Res(f"x_tok{i}") for i in range(2)]
            r_xn_tok = [Res("xn_tok0")]
            r_xn_fm = [Res(f"xn_fm{i}") for i in range(2)]
            r_outb = [Res("outb0")]
            r_S1 = [[Res(f"S1_{i}_{c}") for c in range(CC)] for i in range(2)]
            r_S3 = [Res(f"S3_{i}") for i in range(2)]
            r_S4 = [[Res(f"S4_{i}_{c}") for c in range(CC)] for i in range(2)]
            r_S5, r_S6 = Res("S5"), Res("S6")
            r_s5 = [[Res(f"s5_{i}_{c}") for c in range(CC)] for i in range(2)]
            r_s6 = [[Res(f"s6_{i}_{c}") for c in range(CC)] for i in range(2)]
            r_zNb = [Res(f"zNb{i}") for i in range(2)]
            r_zDb = [Res(f"zDb{i}") for i in range(2)]
            r_ones = Res("onesb")
            r_mv = [Res(f"mv{i}") for i in range(2)]
            r_stt = [Res(f"stt{i}") for i in range(2)]
            r_rstd = [Res(f"rstd{i}") for i in range(2)]
            r_lnv = [Res(f"lnv{i}") for i in range(2)]
            r_pp_in = [Res(f"pp_in{i}") for i in range(2)]
            r_psT = [Res(f"psT{i}") for i in range(2)]
            r_psP = [Res(f"psP{i}") for i in range(2)]
            r_psK = [Res(f"psK{i}") for i in range(2)]
            r_psR = [Res(f"psR{i}") for i in range(2)]
            r_krtok = [Res(f"krtok{i}") for i in range(2)]

            bd.op("sync", lambda e: e.dma_start(out=ident[:], in_=id_d.ap()),
                  writes=[r_ident], dma_sem="d_setup")
            bd.op("sync", lambda e: e.dma_start(out=tdv[:], in_=tdp_d.ap()),
                  writes=[r_tdv], dma_sem="d_setup")
            bd.op("sync", lambda e: e.dma_start(out=tfv[:], in_=tfp_d.ap()),
                  writes=[r_tfv], dma_sem="d_setup")
            for i, v in [(0, EPS), (1, -0.5), (2, -1.0), (5, 127.0)]:
                bd.op("gpsimd", lambda e, i=i, v=v: e.memset(cst[:, i:i + 1], v),
                      writes=[r_cst])
            bd.op("vector", lambda e: e.reciprocal(cst[:, 3:4], cst[:, 5:6]),
                  reads=[r_cst], writes=[r_cst])
            bd.op("vector", lambda e: e.tensor_tensor(
                out=cst[:, 4:5], in0=cst[:, 3:4], in1=cst[:, 3:4], op=ALU.mult),
                reads=[r_cst], writes=[r_cst])
            bd.op("scalar", lambda e: e.activation(out=euv[:], in_=tfv[:], func=AF.Exp),
                  reads=[r_tfv], writes=[r_euv])
            bd.op("scalar", lambda e: e.activation(out=wneg[:], in_=tdv[:], func=AF.Exp),
                  reads=[r_tdv], writes=[r_wneg])
            bd.op("scalar", lambda e: e.activation(out=dv[:], in_=wneg[:], func=AF.Exp,
                                                   scale=cst[:, 2:3]),
                  reads=[r_wneg, r_cst], writes=[r_dv])
            zN3 = [zNb[i][:].rearrange("p (c t) -> p c t", t=TT1 + 1) for i in range(2)]
            zD3 = [zDb[i][:].rearrange("p (c t) -> p c t", t=TT1 + 1) for i in range(2)]
            bd.op("vector", lambda e: e.memset(zN3[1][:, :, TT1:TT1 + 1], 0.0), writes=[r_zNb[1]])
            bd.op("vector", lambda e: e.memset(zD3[1][:, :, TT1:TT1 + 1], 0.0), writes=[r_zDb[1]])
            bd.op("vector", lambda e: e.memset(zN3[0][:, :, 0:1], 0.0), writes=[r_zNb[0]])
            bd.op("vector", lambda e: e.memset(zD3[0][:, :, 0:1], 0.0), writes=[r_zDb[0]])

            def pp_mat_unit(src_d, row_cc, dstT, r_dstT, nrowchunks=CC):
                sl = pp_mat_unit.ctr % 2
                pp_mat_unit.ctr += 1
                src_ap = dram_ap(src_d, row_cc * P * C, [[C, P], [1, C]])

                def dma_part(sl=sl, src_ap=src_ap):
                    bd.op("gpsimd",
                          lambda e, sl=sl, src_ap=src_ap: e.dma_start(
                              out=pp_in[sl][:, 0:C], in_=src_ap),
                          writes=[r_pp_in[sl]], dma_sem=f"d_pp{sl}")

                def cmp_part(sl=sl, row_cc=row_cc):
                    def tr(e, sl=sl):
                        last = None
                        for cc in range(CC):
                            last = e.transpose(psP[sl][:, cc * P:(cc + 1) * P],
                                               pp_in[sl][:, cc * P:(cc + 1) * P], ident[:])
                        return last
                    bd.op("tensor", tr, reads=[r_pp_in[sl], r_ident], writes=[r_psP[sl]])
                    dst3 = dstT[:].rearrange("p (a b) -> p a b", b=C)
                    bd.op("scalar",
                          lambda e, sl=sl, row_cc=row_cc: e.copy(
                              dst3[:, :, row_cc * P:(row_cc + 1) * P], psP[sl][:, 0:C]),
                          reads=[r_psP[sl]], writes=[r_dstT])
                return dma_part, cmp_part, sl
            pp_mat_unit.ctr = 0

            def pp_wgu_unit(src_d, hc, iu):
                sl = pp_mat_unit.ctr % 2
                pp_mat_unit.ctr += 1
                ssl = pp_mat_unit.ctr % 2
                src_ap = dram_ap(src_d, hc * P * C, [[C, P], [1, C]])

                def dma_part(sl=sl, src_ap=src_ap):
                    bd.op("gpsimd",
                          lambda e, sl=sl, src_ap=src_ap: e.dma_start(
                              out=pp_in[sl][:, 0:C], in_=src_ap),
                          writes=[r_pp_in[sl]], dma_sem=f"d_pp{sl}")

                def cmp_part(sl=sl, ssl=ssl, hc=hc, iu=iu):
                    def tr(e, sl=sl):
                        last = None
                        for cc in range(CC):
                            last = e.transpose(psP[sl][:, cc * P:(cc + 1) * P],
                                               pp_in[sl][:, cc * P:(cc + 1) * P], ident[:])
                        return last
                    bd.op("tensor", tr, reads=[r_pp_in[sl], r_ident], writes=[r_psP[sl]])
                    bd.op("scalar",
                          lambda e, sl=sl, ssl=ssl: e.copy(pp_stage[ssl][:], psP[sl][:, 0:C]),
                          reads=[r_psP[sl]], writes=[r_pp_stage[ssl]])
                    dst = dram_ap(wgu_s, ((hc * 2 + iu) * CC) * P * P,
                                  [[P, P], [P * P, CC], [1, P]])
                    bd.op("sync",
                          lambda e, ssl=ssl, dst=dst: e.dma_start(out=dst, in_=pp_stage[ssl][:]),
                          reads=[r_pp_stage[ssl]], writes=[r_wgu_s[hc]], dma_sem=f"d_pps{ssl}")
                return dma_part, cmp_part, sl

            def pp_wd_unit(dc):
                sl = 0
                pp_mat_unit.ctr += 1
                ctr = pp_mat_unit.ctr
                src_ap = dram_ap(wd_d, dc * P * H, [[H, P], [1, H]])

                def dma_part(sl=sl, src_ap=src_ap):
                    bd.op("gpsimd",
                          lambda e, sl=sl, src_ap=src_ap: e.dma_start(
                              out=pp_in[sl][:, 0:H], in_=src_ap),
                          writes=[r_pp_in[sl]], dma_sem=f"d_pp{sl}")

                def cmp_part(sl=sl, dc=dc, ctr=ctr):
                    for g in range(3):
                        psl = (sl + g) % 2
                        def tr(e, sl=sl, g=g, psl=psl):
                            last = None
                            for j in range(CC):
                                hc = g * CC + j
                                last = e.transpose(psP[psl][:, j * P:(j + 1) * P],
                                                   pp_in[sl][:, hc * P:(hc + 1) * P], ident[:])
                            return last
                        bd.op("tensor", tr, reads=[r_pp_in[sl], r_ident], writes=[r_psP[psl]])
                        ssl = (ctr + g) % 2
                        bd.op("scalar",
                              lambda e, psl=psl, ssl=ssl: e.copy(pp_stage[ssl][:], psP[psl][:, 0:C]),
                              reads=[r_psP[psl]], writes=[r_pp_stage[ssl]])
                        dst = dram_ap(wdT_s, (g * CC) * P * C + dc * P,
                                      [[C, P], [P * C, CC], [1, P]])
                        bd.op("sync",
                              lambda e, ssl=ssl, dst=dst: e.dma_start(out=dst, in_=pp_stage[ssl][:]),
                              reads=[r_pp_stage[ssl]],
                              writes=[r_wdT_s[g * CC + j] for j in range(CC)],
                              dma_sem=f"d_pps{ssl}")
                return dma_part, cmp_part, 0

            pp_stage = [E1(nc.sbuf_tensor(f"pp_stage{i}", [P, C], BF16)) for i in range(2)]
            r_pp_stage = [Res(f"pp_stage{i}") for i in range(2)]


            startup = [pp_mat_unit(wk_d, cc, wkT, r_wkT) for cc in range(CC)]
            startup += [pp_mat_unit(wr_d, cc, wrT, r_wrT) for cc in range(CC)]
            startup[0][0]()
            startup[1][0]()
            for i, u in enumerate(startup):
                u[1]()
                if i + 2 < len(startup):
                    startup[i + 2][0]()

            pp_units = []
            for cc in range(CC):
                pp_units.append(lambda cc=cc: pp_mat_unit(wo_d, cc, woT, r_woT))
            for hc in range(HC):
                pp_units.append(lambda hc=hc: pp_wgu_unit(wg_d, hc, 0))
                pp_units.append(lambda hc=hc: pp_wgu_unit(wu_d, hc, 1))
            for dc in range(CC):
                pp_units.append(lambda dc=dc: pp_wd_unit(dc))
            pp_idx = 0
            pp_dma_idx = 0
            pp_pairs = []

            def pp_get(i):
                while len(pp_pairs) <= i:
                    pp_pairs.append(pp_units[len(pp_pairs)]())
                return pp_pairs[i]

            def pp_can_dma(j, cmp_done):
                sj = pp_get(j)[2]
                for i in range(j - 1, -1, -1):
                    if pp_get(i)[2] == sj:
                        return i < cmp_done
                return True

            def pp_drip(n_units):
                nonlocal pp_idx, pp_dma_idx
                while pp_idx < n_units:
                    while pp_dma_idx <= pp_idx:
                        pp_get(pp_dma_idx)[0]()
                        pp_dma_idx += 1
                    pp_get(pp_idx)[1]()
                    pp_idx += 1
                    while (pp_dma_idx < min(pp_idx + 2, len(pp_units))
                           and pp_can_dma(pp_dma_idx, pp_idx)):
                        pp_get(pp_dma_idx)[0]()
                        pp_dma_idx += 1

            def stage_a_dma(ch):
                t0 = ch * TT1
                sl = ch % 2
                src = dram_ap(x_d, t0 * C, [[C, P], [P * C, TS1], [1, C]])
                bd.op("sync",
                      lambda e, sl=sl, src=src: e.dma_start(out=x_tok[sl][:], in_=src),
                      writes=[r_x_tok[sl]], dma_sem=f"d_x{sl}")

            def stage_a_stats(ch):
                sl = ch % 2
                for ts in range(TS1):
                    for hh in range(2):
                        bd.op("vector",
                              lambda e, sl=sl, ts=ts, hh=hh: e.bn_stats(
                                  stt_b[sl][:, (ts * 2 + hh) * 6:(ts * 2 + hh + 1) * 6],
                                  x_tok[sl][:, ts * C + hh * 512: ts * C + (hh + 1) * 512]),
                              reads=[r_x_tok[sl]], writes=[r_stt[sl]])
                    bd.op("vector",
                          lambda e, sl=sl, ts=ts: e.bn_aggr(
                              mvb[sl][:, ts * 2:(ts + 1) * 2],
                              stt_b[sl][:, ts * 12:(ts * 12) + 12]),
                          reads=[r_stt[sl]], writes=[r_mv[sl]])
                    bd.op("scalar",
                          lambda e, sl=sl, ts=ts: e.activation(
                              out=lnv[sl][:, ts:ts + 1], in_=mvb[sl][:, ts * 2 + 1:ts * 2 + 2],
                              func=AF.Ln, bias=cst[:, 0:1]),
                          reads=[r_mv[sl], r_cst], writes=[r_lnv[sl]])
                    bd.op("scalar",
                          lambda e, sl=sl, ts=ts: e.activation(
                              out=rstd[sl][:, ts:ts + 1], in_=lnv[sl][:, ts:ts + 1],
                              func=AF.Exp, scale=cst[:, 1:2]),
                          reads=[r_lnv[sl], r_cst], writes=[r_rstd[sl]])

            def stage_a_apply(ch):
                sl = ch % 2
                for ts in range(TS1):
                    bd.op("vector",
                          lambda e, sl=sl, ts=ts: e.tensor_scalar(
                              out=xn_tok[0][:, ts * C:(ts + 1) * C],
                              in0=x_tok[sl][:, ts * C:(ts + 1) * C],
                              scalar1=mvb[sl][:, ts * 2:ts * 2 + 1], op0=ALU.subtract,
                              scalar2=rstd[sl][:, ts:ts + 1], op1=ALU.mult),
                          reads=[r_x_tok[sl], r_mv[sl], r_rstd[sl]], writes=[r_xn_tok[0]])

            def stage_a_post(ch):
                t0 = ch * TT1
                sl = ch % 2
                for ts in range(TS1):
                    def tr(e, sl=sl, ts=ts):
                        last = None
                        for cc in range(CC):
                            last = e.transpose(psT[ts][:, cc * P:(cc + 1) * P],
                                               xn_tok[0][:, ts * C + cc * P: ts * C + (cc + 1) * P],
                                               ident[:])
                        return last
                    bd.op("tensor", tr, reads=[r_xn_tok[0], r_ident], writes=[r_psT[ts]])
                    xnf3 = xn_fm[sl][:].rearrange("p (a b) -> p a b", b=TT1)
                    bd.op("scalar",
                          lambda e, sl=sl, ts=ts, xnf3=xnf3: e.copy(
                              xnf3[:, :, ts * P:(ts + 1) * P], psT[ts][:, 0:C]),
                          reads=[r_psT[ts]], writes=[r_xn_fm[sl]])
                dst = dram_ap(xn_s, t0, [[T, P], [P * T, CC], [1, TT1]])
                bd.op("sync", lambda e, sl=sl, dst=dst: e.dma_start(out=dst, in_=xn_fm[sl][:]),
                      reads=[r_xn_fm[sl]], writes=[r_xn_s[ch]], dma_sem=f"d_xns{sl}")

            def stage_b(ch):
                sl = ch % 2
                for pr in range(CC // 2):
                    pk = pr % 2
                    for half in range(2):
                        cc_o = pr * 2 + half
                        for (wT, r_wT, psx, r_psx) in ((wkT, r_wkT, psK, r_psK),
                                                       (wrT, r_wrT, psR, r_psR)):
                            def mm(e, sl=sl, cc_o=cc_o, wT=wT, psx=psx, pk=pk, half=half):
                                last = None
                                for cc_i in range(CC):
                                    last = e.matmul(
                                        psx[pk][:, half * TT1:(half + 1) * TT1],
                                        wT[:, cc_i * C + cc_o * P: cc_i * C + (cc_o + 1) * P],
                                        xn_fm[sl][:, cc_i * TT1:(cc_i + 1) * TT1],
                                        start=(cc_i == 0), stop=(cc_i == CC - 1),
                                        skip_group_check=True)
                                return last
                            bd.op("tensor", mm, reads=[r_wT, r_xn_fm[sl]], writes=[r_psx[pk]])
                    c0 = pr * 2
                    bd.op("scalar",
                          lambda e, sl=sl, c0=c0, pk=pk: e.activation(
                              out=S1[sl][:, c0 * TT1:(c0 + 2) * TT1],
                              in_=psK[pk][:], func=AF.Exp),
                          reads=[r_psK[pk]], writes=[r_S1[sl][c0], r_S1[sl][c0 + 1]])
                    bd.op("scalar",
                          lambda e, sl=sl, c0=c0, pk=pk: e.activation(
                              out=S3[sl][:, c0 * TT1:(c0 + 2) * TT1],
                              in_=psR[pk][:], func=AF.Exp, scale=cst[:, 2:3]),
                          reads=[r_psR[pk], r_cst], writes=[r_S3[sl], r_krtok[pk]])
                    bd.op("vector",
                          lambda e, sl=sl, c0=c0, pk=pk: e.tensor_tensor(
                              out=S4[sl][:, c0 * TT1:(c0 + 2) * TT1],
                              in0=S1[sl][:, c0 * TT1:(c0 + 2) * TT1],
                              in1=psR[pk][:], op=ALU.mult),
                          reads=[r_S1[sl][c0], r_S1[sl][c0 + 1], r_psR[pk],
                                 r_krtok[pk]],
                          writes=[r_S4[sl][c0], r_S4[sl][c0 + 1]])
                if ch > 0:
                    ol = 1 - sl
                    bd.op("scalar",
                          lambda e, sl=sl, ol=ol: e.copy(zN3[sl][:, :, 0:1],
                                                         zN3[ol][:, :, TT1:TT1 + 1]),
                          reads=[r_zNb[ol]], writes=[r_zNb[sl]])
                    bd.op("scalar",
                          lambda e, sl=sl, ol=ol: e.copy(zD3[sl][:, :, 0:1],
                                                         zD3[ol][:, :, TT1:TT1 + 1]),
                          reads=[r_zDb[ol]], writes=[r_zDb[sl]])

            def stage_c_scans(ch, lo, hi):
                sl = ch % 2
                ol = 1 - sl
                for cc in range(lo, hi):
                    b0 = cc * (TT1 + 1)
                    bd.op("vector",
                          lambda e, sl=sl, ol=ol, cc=cc, b0=b0: e.tensor_tensor_scan(
                              out=zDb[sl][:, b0 + 1: b0 + 1 + TT1],
                              data0=bass.AP(dv, cc, [[CC, P], [0, TT1]]),
                              data1=S1[sl][:, cc * TT1:(cc + 1) * TT1],
                              initial=zDb[ol][:, b0 + TT1: b0 + TT1 + 1],
                              op0=ALU.mult, op1=ALU.add),
                          reads=[r_dv, r_S1[sl][cc], r_zDb[ol]], writes=[r_zDb[sl]])
                    bd.op("vector",
                          lambda e, sl=sl, ol=ol, cc=cc, b0=b0: e.tensor_tensor_scan(
                              out=zNb[sl][:, b0 + 1: b0 + 1 + TT1],
                              data0=bass.AP(dv, cc, [[CC, P], [0, TT1]]),
                              data1=S4[sl][:, cc * TT1:(cc + 1) * TT1],
                              initial=zNb[ol][:, b0 + TT1: b0 + TT1 + 1],
                              op0=ALU.mult, op1=ALU.add),
                          reads=[r_dv, r_S4[sl][cc], r_zNb[ol]], writes=[r_zNb[sl]])

            def stage_c_stt(ch):
                sl = ch % 2
                for cc in range(CC):
                    sli = slice(cc * TT1, (cc + 1) * TT1)
                    bd.op("vector",
                          lambda e, sl=sl, cc=cc, sli=sli: e.scalar_tensor_tensor(
                              out=S5[sl][:, sli], in0=S1[sl][:, sli],
                              scalar=euv[:, cc:cc + 1],
                              in1=zD3[sl][:, cc, 0:TT1], op0=ALU.mult, op1=ALU.add),
                          reads=[r_S1[sl][cc], r_euv, r_zDb[sl]], writes=[r_s5[sl][cc]])
                    bd.op("vector",
                          lambda e, sl=sl, sli=sli, cc=cc: e.scalar_tensor_tensor(
                              out=S6[sl][:, sli], in0=S3[sl][:, sli], scalar=1.0,
                              in1=S5[sl][:, sli], op0=ALU.add, op1=ALU.mult),
                          reads=[r_S3[sl], r_s5[sl][cc]], writes=[r_s6[sl][cc]])
            def stage_d(ch):
                sl = ch % 2
                for cc in range(CC):
                    sli = slice(cc * TT1, (cc + 1) * TT1)
                    bd.op("scalar",
                          lambda e, sl=sl, sli=sli: e.activation(
                              out=S5[sl][:, sli], in_=S6[sl][:, sli], func=AF.Ln),
                          reads=[r_s6[sl][cc]], writes=[r_s5[sl][cc]])
                    bd.op("scalar",
                          lambda e, sl=sl, sli=sli: e.activation(
                              out=S6[sl][:, sli], in_=S5[sl][:, sli],
                              func=AF.Exp, scale=cst[:, 2:3]),
                          reads=[r_s5[sl][cc], r_cst], writes=[r_s6[sl][cc]])

            def stage_e(ch):
                t0 = ch * TT1
                sl = ch % 2
                for cc in range(CC):
                    bd.op("vector",
                          lambda e, sl=sl, cc=cc: e.scalar_tensor_tensor(
                              out=S5[sl][:, cc * TT1:(cc + 1) * TT1],
                              in0=S4[sl][:, cc * TT1:(cc + 1) * TT1],
                              scalar=euv[:, cc:cc + 1],
                              in1=zN3[sl][:, cc, 0:TT1], op0=ALU.mult, op1=ALU.add),
                          reads=[r_S4[sl][cc], r_euv, r_zNb[sl]],
                          writes=[r_s5[sl][cc]])
                bd.op("gpsimd", lambda e, sl=sl: e.tensor_tensor(
                    out=outb[0][:], in0=S5[sl][:], in1=S6[sl][:], op=ALU.mult),
                    reads=[r_s5[sl][cc] for cc in range(CC)]
                          + [r_s6[sl][cc] for cc in range(CC)],
                    writes=[r_outb[0]])
                dst = dram_ap(outp_s, t0, [[T, P], [P * T, CC], [1, TT1]])
                bd.op("sync", lambda e, dst=dst: e.dma_start(out=dst, in_=outb[0][:]),
                      reads=[r_outb[0]], writes=[r_out_s[ch]], dma_sem="d_ots0")

            stage_a_dma(0)
            stage_a_dma(1)
            stage_a_stats(0)
            stage_a_apply(0)
            stage_a_post(0)
            stage_b(0)
            stage_a_dma(2)
            stage_a_stats(1)
            stage_a_apply(1)
            stage_a_post(1)
            while (pp_dma_idx < min(2, len(pp_units))
                   and pp_can_dma(pp_dma_idx, pp_idx)):
                pp_get(pp_dma_idx)[0]()
                pp_dma_idx += 1
            for ch in range(NC1):
                if ch + 3 < NC1:
                    stage_a_dma(ch + 3)
                stage_c_scans(ch, 0, CC)
                stage_c_stt(ch)
                if ch > 0:
                    stage_e(ch - 1)
                if ch + 1 < NC1:
                    stage_b(ch + 1)
                if ch + 2 < NC1:
                    stage_a_stats(ch + 2)
                    stage_a_apply(ch + 2)
                    stage_a_post(ch + 2)
                pp_drip((len(pp_units) * (ch + 1)) // NC1)
                stage_d(ch)
            stage_e(NC1 - 1)

            pp_drip(len(pp_units))

            bd.finalize(block)

        with contextlib.ExitStack() as ph2:
            E2 = ph2.enter_context
            xnc = [E2(nc.sbuf_tensor(f"xnc{i}", [P, CC * TT2], BF16)) for i in range(2)]
            otc = [E2(nc.sbuf_tensor(f"otc{i}", [P, CC * TT2], BF16)) for i in range(2)]
            x2 = [E2(nc.sbuf_tensor(f"x2_{i}", [P, TS2 * C], F32)) for i in range(2)]
            res_t = [E2(nc.sbuf_tensor(f"res{i}", [P, TS2 * C], F32)) for i in range(2)]
            h_sb = E2(nc.sbuf_tensor("h_sb", [P, HC * TT2], BF16))
            wgu_t = [E2(nc.sbuf_tensor(f"wgu{i}", [P, 2 * CC * P], BF16)) for i in range(4)]
            wd_t = [E2(nc.sbuf_tensor(f"wdt{i}", [P, C], BF16)) for i in range(4)]
            sil_b = [E2(nc.sbuf_tensor(f"sil{i}", [P, TT2], F32)) for i in range(2)]

            psA = [E2(nc.psum_tensor(f"psA{i}", [P, TT2], F32)) for i in range(2)]
            psB = [E2(nc.psum_tensor(f"psB{i}", [P, TT2], F32)) for i in range(2)]
            psRb = [E2(nc.psum_tensor(f"psRb{i}", [P, TT2], F32)) for i in range(4)]

            r_xnc = [Res(f"xnc{i}") for i in range(2)]
            r_otc = [Res(f"otc{i}") for i in range(2)]
            r_x2 = [Res(f"x2_{i}") for i in range(2)]
            r_res = [Res(f"res{i}") for i in range(2)]
            r_h = Res("h_sb")
            r_wgu = [Res(f"wgu{i}") for i in range(4)]
            r_wd = [Res(f"wdt{i}") for i in range(4)]
            r_sil = [Res(f"sil{i}") for i in range(2)]
            r_psA = [Res(f"psA{i}") for i in range(2)]
            r_psB = [Res(f"psB{i}") for i in range(2)]
            r_psRb = [Res(f"psRb{i}") for i in range(4)]
            banks = [r_psRb[0], r_psRb[1], r_psRb[2], r_psRb[3],
                     r_psA[0], r_psA[1], r_psB[0], r_psB[1]]
            bank_t = [psRb[0], psRb[1], psRb[2], psRb[3], psA[0], psA[1], psB[0], psB[1]]

            bd.fence()

            for ch in range(NC2):
                t0 = ch * TT2
                sl = ch % 2
                src = dram_ap(xn_s, t0, [[T, P], [P * T, CC], [1, TT2]])
                bd.op("sync", lambda e, sl=sl, src=src: e.dma_start(out=xnc[sl][:], in_=src),
                      reads=[r_xn_s[2 * ch], r_xn_s[2 * ch + 1]],
                      writes=[r_xnc[sl]], dma_sem=f"d_xnc{sl}")
                src = dram_ap(outp_s, t0, [[T, P], [P * T, CC], [1, TT2]])
                bd.op("sync", lambda e, sl=sl, src=src: e.dma_start(out=otc[sl][:], in_=src),
                      reads=[r_out_s[2 * ch], r_out_s[2 * ch + 1]],
                      writes=[r_otc[sl]], dma_sem=f"d_otc{sl}")
                src = dram_ap(x_d, t0 * C, [[C, P], [P * C, TS2], [1, C]])
                bd.op("sync", lambda e, sl=sl, src=src: e.dma_start(out=x2[sl][:], in_=src),
                      writes=[r_x2[sl]], dma_sem=f"d_x2{sl}")

                for hc in range(HC):
                    wsl = hc % 4
                    src = dram_ap(wgu_s, hc * 2 * CC * P * P,
                                  [[P, P], [CC * P * P, 2], [P * P, CC], [1, P]])
                    bd.op("sync",
                          lambda e, wsl=wsl, src=src: e.dma_start(out=wgu_t[wsl][:], in_=src),
                          reads=[r_wgu_s[hc]], writes=[r_wgu[wsl]], dma_sem=f"d_wgu{wsl}")

                    pab = hc % 2
                    def mmg(e, sl=sl, wsl=wsl, pab=pab, iu=0):
                        last = None
                        for cc in range(CC):
                            last = e.matmul(
                                psA[pab][:],
                                wgu_t[wsl][:, (iu * CC + cc) * P:(iu * CC + cc + 1) * P],
                                xnc[sl][:, cc * TT2:(cc + 1) * TT2],
                                start=(cc == 0), stop=(cc == CC - 1))
                        return last
                    bd.op("tensor", mmg, reads=[r_wgu[wsl], r_xnc[sl]], writes=[r_psA[pab]])

                    def mmu(e, sl=sl, wsl=wsl, pab=pab, iu=1):
                        last = None
                        for cc in range(CC):
                            last = e.matmul(
                                psB[pab][:],
                                wgu_t[wsl][:, (iu * CC + cc) * P:(iu * CC + cc + 1) * P],
                                xnc[sl][:, cc * TT2:(cc + 1) * TT2],
                                start=(cc == 0), stop=(cc == CC - 1))
                        return last
                    bd.op("tensor", mmu, reads=[r_wgu[wsl], r_xnc[sl]], writes=[r_psB[pab]])
                    bd.op("scalar",
                          lambda e, pab=pab: e.activation(
                              out=sil_b[pab][:], in_=psA[pab][:], func=AF.Silu,
                              scale=cst[:, 3:4]),
                          reads=[r_psA[pab], r_cst], writes=[r_sil[pab]])
                    bd.op("vector",
                          lambda e, hc=hc, pab=pab: e.scalar_tensor_tensor(
                              out=h_sb[:, hc * TT2:(hc + 1) * TT2],
                              in0=sil_b[pab][:], scalar=cst[:, 4:5], in1=psB[pab][:],
                              op0=ALU.mult, op1=ALU.mult),
                          reads=[r_sil[pab], r_psB[pab]], writes=[r_h])

                for hc in range(HC):
                    wsl = hc % 4
                    src = dram_ap(wdT_s, hc * P * C, [[C, P], [1, C]])
                    bd.op("sync",
                          lambda e, wsl=wsl, src=src: e.dma_start(out=wd_t[wsl][:], in_=src),
                          reads=[r_wdT_s[hc]], writes=[r_wd[wsl]], dma_sem=f"d_wd{wsl}")

                    def mmd(e, hc=hc, wsl=wsl):
                        last = None
                        for ts in range(TS2):
                            for dh in range(2):
                                last = e.matmul(
                                    bank_t[ts * 2 + dh][:],
                                    h_sb[:, hc * TT2 + ts * P: hc * TT2 + (ts + 1) * P],
                                    wd_t[wsl][:, dh * 512:(dh + 1) * 512],
                                    start=(hc == 0), stop=False, skip_group_check=True)
                        return last
                    bd.op("tensor", mmd, reads=[r_h, r_wd[wsl]], writes=banks)
                for hc2 in range(CC):
                    def mmw(e, sl=sl, hc2=hc2):
                        last = None
                        for ts in range(TS2):
                            for dh in range(2):
                                last = e.matmul(
                                    bank_t[ts * 2 + dh][:],
                                    otc[sl][:, hc2 * TT2 + ts * P: hc2 * TT2 + (ts + 1) * P],
                                    woT[:, hc2 * C + dh * 512: hc2 * C + (dh + 1) * 512],
                                    start=False, stop=(hc2 == CC - 1), skip_group_check=True)
                        return last
                    bd.op("tensor", mmw, reads=[r_otc[sl], r_woT], writes=banks)
                for grp in range(2 * TS2):
                    bd.op("vector",
                          lambda e, sl=sl, grp=grp: e.tensor_tensor(
                              out=res_t[sl][:, grp * 512:(grp + 1) * 512],
                              in0=bank_t[grp][:],
                              in1=x2[sl][:, grp * 512:(grp + 1) * 512], op=ALU.add),
                          reads=[banks[grp], r_x2[sl]], writes=[r_res[sl]])
                dst = dram_ap(out_d, t0 * C, [[C, P], [P * C, TS2], [1, C]])
                bd.op("gpsimd", lambda e, sl=sl, dst=dst: e.dma_start(out=dst, in_=res_t[sl][:]),
                      reads=[r_res[sl]], dma_sem=f"d_rs{sl}")

            bd.fence(("sync", "gpsimd"))
            bd.finalize(block)

    return nc


_PROG_CACHE = {}


def kernel(L0_attn_x, L0_attn_Wk, L0_attn_Wr, L0_attn_Wo,
           L0_attn_time_decay, L0_attn_time_first,
           L0_ffn_w_gate, L0_ffn_w_up, L0_ffn_w_down):
    x = np.ascontiguousarray(np.asarray(L0_attn_x, dtype=np.float32))
    B, T, C_ = x.shape
    wk = np.ascontiguousarray(np.asarray(L0_attn_Wk, dtype=np.float32))
    wr = np.ascontiguousarray(np.asarray(L0_attn_Wr, dtype=np.float32))
    wo = np.ascontiguousarray(np.asarray(L0_attn_Wo, dtype=np.float32))
    wg = np.ascontiguousarray(np.asarray(L0_ffn_w_gate, dtype=np.int32))
    wu = np.ascontiguousarray(np.asarray(L0_ffn_w_up, dtype=np.int32))
    wd = np.ascontiguousarray(np.asarray(L0_ffn_w_down, dtype=np.int32))
    td = np.asarray(L0_attn_time_decay, dtype=np.float32)
    tf = np.asarray(L0_attn_time_first, dtype=np.float32)
    tdp = np.ascontiguousarray(td.reshape(CC, P).T)
    tfp = np.ascontiguousarray(tf.reshape(CC, P).T)
    ident = np.eye(P, dtype=ml_dtypes.bfloat16)

    if T not in _PROG_CACHE:
        _PROG_CACHE[T] = build_program(T)
    nc = _PROG_CACHE[T]

    in_maps = []
    for b in range(B):
        in_maps.append(dict(x=np.ascontiguousarray(x[b]), Wk=wk, Wr=wr, Wo=wo,
                            wg=wg, wu=wu, wd=wd, tdp=tdp, tfp=tfp, ident=ident))
    res = run_bass_kernel_spmd(nc, in_maps, core_ids=list(range(B)))
    out = np.stack([np.asarray(r["out"], dtype=np.float32) for r in res.results], axis=0)
    return out


# version 33
# speedup vs baseline: 1.1413x; 1.0502x over previous
import contextlib
import numpy as np
import ml_dtypes
import concourse.bass as bass
import concourse.mybir as mybir
from concourse.bass_utils import run_bass_kernel_spmd

F32 = mybir.dt.float32
BF16 = mybir.dt.bfloat16
I32 = mybir.dt.int32
AF = mybir.ActivationFunctionType
ALU = mybir.AluOpType

P = 128
C = 1024
H = 3072
CC = C // P
HC = H // P
EPS = 1e-5
INV127 = 1.0 / 127.0
INV127SQ = INV127 * INV127

ENGINES = ("tensor", "vector", "scalar", "gpsimd", "sync")
ENG_SEM = {"tensor": "pe", "vector": "dve", "scalar": "act", "gpsimd": "pool"}


class Res:
    __slots__ = ("writer", "readers", "name")

    def __init__(self, name):
        self.writer = None
        self.readers = []
        self.name = name


class Build:

    def __init__(self, nc):
        self.nc = nc
        self.q = {e: [] for e in ENGINES}
        self.sems = {}
        self.semv = {}
        self.waited = {e: {} for e in ENGINES}

    def add_sem(self, name, handle):
        self.sems[name] = handle
        self.semv[name] = 0

    def op(self, eng, fn, reads=(), writes=(), dma_sem=None):
        edges = {}

        def add(t):
            if t is not None:
                s, v = t
                if edges.get(s, 0) < v:
                    edges[s] = v

        for r in reads:
            add(r.writer)
        for w in writes:
            add(w.writer)
            for t in w.readers:
                add(t)
        if dma_sem is not None:
            add((dma_sem, self.semv[dma_sem]))
        wl = []
        for s, v in sorted(edges.items()):
            if self.waited[eng].get(s, 0) >= v:
                continue
            wl.append((s, v))
            self.waited[eng][s] = v
        sem = dma_sem if dma_sem is not None else ENG_SEM[eng]
        inc = 16 if dma_sem is not None else 1
        self.semv[sem] += inc
        tick = (sem, self.semv[sem])

        def emit(e, wl=wl, fn=fn, sem=sem, inc=inc):
            for s, v in wl:
                e.wait_ge(self.sems[s], v)
            inst = fn(e)
            inst.then_inc(self.sems[sem], inc)

        self.q[eng].append(emit)
        for w in writes:
            w.writer = tick
            w.readers = []
        for r in reads:
            r.readers.append(tick)
        return tick

    def raw_wait(self, eng, sem, val):
        def emit(e, sem=sem, val=val):
            e.wait_ge(self.sems[sem], val)
        self.q[eng].append(emit)

    def fence(self, engs=ENGINES):
        for eng in engs:
            for s, v in list(self.semv.items()):
                if v == 0:
                    continue
                if self.waited[eng].get(s, 0) >= v:
                    continue
                self.raw_wait(eng, s, v)
                self.waited[eng][s] = v

    def finalize(self, block):
        for eng in ENGINES:
            ops = self.q[eng]
            if not ops:
                continue
            section = getattr(block, eng)

            def body(e, ops=ops):
                for f in ops:
                    f(e)
            section(body)
            self.q[eng] = []


def dram_ap(handle, offset, dims):
    return bass.AP(handle, offset, [list(d) for d in dims])


def build_program(T=4096):
    TT1 = 256
    NC1 = T // TT1
    TS1 = TT1 // P
    TT2 = 512
    NC2 = T // TT2
    TS2 = TT2 // P

    nc = bass.Bass("TRN2", target_bir_lowering=False)

    x_d = nc.dram_tensor("x", [T, C], F32, kind="ExternalInput")
    wk_d = nc.dram_tensor("Wk", [C, C], F32, kind="ExternalInput")
    wr_d = nc.dram_tensor("Wr", [C, C], F32, kind="ExternalInput")
    wo_d = nc.dram_tensor("Wo", [C, C], F32, kind="ExternalInput")
    wg_d = nc.dram_tensor("wg", [H, C], I32, kind="ExternalInput")
    wu_d = nc.dram_tensor("wu", [H, C], I32, kind="ExternalInput")
    wd_d = nc.dram_tensor("wd", [C, H], I32, kind="ExternalInput")
    tdp_d = nc.dram_tensor("tdp", [P, CC], F32, kind="ExternalInput")
    tfp_d = nc.dram_tensor("tfp", [P, CC], F32, kind="ExternalInput")
    id_d = nc.dram_tensor("ident", [P, P], BF16, kind="ExternalInput")
    out_d = nc.dram_tensor("out", [T, C], F32, kind="ExternalOutput")

    wgu_s = nc.dram_tensor("wgu_s", [HC, 2, CC, P, P], BF16, kind="Internal")
    wdT_s = nc.dram_tensor("wdT_s", [HC, P, C], BF16, kind="Internal")
    xn_s = nc.dram_tensor("xn_s", [CC, P, T], BF16, kind="Internal")
    outp_s = nc.dram_tensor("outp_s", [CC, P, T], BF16, kind="Internal")

    bd = Build(nc)

    with contextlib.ExitStack() as top:
        E = top.enter_context
        for s in ("pe", "dve", "act", "pool"):
            bd.add_sem(s, E(nc.semaphore(s)))
        dma_sems = [
            "d_setup",
            "d_x0", "d_x1",
            "d_xns0", "d_xns1",
            "d_ots0", "d_ots1",
            "d_pp0", "d_pp1",
            "d_pps0", "d_pps1",
            "d_xnc0", "d_xnc1",
            "d_otc0", "d_otc1",
            "d_x20", "d_x21",
            "d_wgu0", "d_wgu1", "d_wgu2", "d_wgu3",
            "d_wd0", "d_wd1", "d_wd2", "d_wd3",
            "d_rs0", "d_rs1",
        ]
        for s in dma_sems:
            bd.add_sem(s, E(nc.semaphore(s)))

        block = E(nc.Block())

        ident = E(nc.sbuf_tensor("identsb", [P, P], BF16))
        woT = E(nc.sbuf_tensor("woT", [P, CC * C], BF16))
        cst = E(nc.sbuf_tensor("cst", [P, 8], F32))
        euv = E(nc.sbuf_tensor("euv", [P, CC], F32))
        dv = E(nc.sbuf_tensor("dv", [P, CC], F32))
        tdv = E(nc.sbuf_tensor("tdv", [P, CC], F32))
        tfv = E(nc.sbuf_tensor("tfv", [P, CC], F32))
        wneg = E(nc.sbuf_tensor("wneg", [P, CC], F32))

        r_ident = Res("ident")
        r_woT = Res("woT")
        r_cst = Res("cst")
        r_euv = Res("euv")
        r_dv = Res("dv")
        r_tdv = Res("tdv")
        r_tfv = Res("tfv")
        r_wneg = Res("wneg")
        r_wgu_s = [Res(f"wgu_s{hc}") for hc in range(HC)]
        r_wdT_s = [Res(f"wdT_s{hc}") for hc in range(HC)]
        r_xn_s = [Res(f"xn_s{ch}") for ch in range(NC1)]
        r_out_s = [Res(f"out_s{ch}") for ch in range(NC1)]

        with contextlib.ExitStack() as ph1:
            E1 = ph1.enter_context
            wkT = E1(nc.sbuf_tensor("wkT", [P, CC * C], BF16))
            wrT = E1(nc.sbuf_tensor("wrT", [P, CC * C], BF16))
            x_tok = [E1(nc.sbuf_tensor("x_tok0", [P, TS1 * C], F32))]
            xn_tok = [E1(nc.sbuf_tensor("xn_tok0", [P, TS1 * C], BF16))]
            xn_fm = [E1(nc.sbuf_tensor(f"xn_fm{i}", [P, CC * TT1], BF16)) for i in range(2)]
            outb = [E1(nc.sbuf_tensor("outb0", [P, CC * TT1], BF16))]
            S1 = [E1(nc.sbuf_tensor(f"S1_{i}", [P, CC * TT1], F32)) for i in range(2)]
            S3 = [E1(nc.sbuf_tensor(f"S3_{i}", [P, CC * TT1], F32)) for i in range(2)]
            S4 = [E1(nc.sbuf_tensor(f"S4_{i}", [P, CC * TT1], F32)) for i in range(2)]
            S5 = [E1(nc.sbuf_tensor(f"S5_{i}", [P, CC * TT1], F32)) for i in range(2)]
            S6 = [E1(nc.sbuf_tensor(f"S6_{i}", [P, CC * TT1], F32)) for i in range(2)]
            zNb = [E1(nc.sbuf_tensor(f"zNb{i}", [P, CC * (TT1 + 1)], F32)) for i in range(2)]
            zDb = [E1(nc.sbuf_tensor(f"zDb{i}", [P, CC * (TT1 + 1)], F32)) for i in range(2)]
            mvb = [E1(nc.sbuf_tensor(f"mvb{i}", [P, 4 * TS1], F32)) for i in range(2)]
            stt_b = [E1(nc.sbuf_tensor(f"sttb{i}", [P, 12 * TS1], F32)) for i in range(2)]
            rstd = [E1(nc.sbuf_tensor(f"rstd{i}", [P, TS1], F32)) for i in range(2)]
            lnv = [E1(nc.sbuf_tensor(f"lnv{i}", [P, TS1], F32)) for i in range(2)]
            pp_in = [E1(nc.sbuf_tensor("pp_in0", [P, H], BF16)),
                     E1(nc.sbuf_tensor("pp_in1", [P, C], BF16))]

            psT = [E1(nc.psum_tensor(f"psT{i}", [P, C], BF16)) for i in range(2)]
            psP = [E1(nc.psum_tensor(f"psP{i}", [P, C], BF16)) for i in range(2)]
            psK = [E1(nc.psum_tensor(f"psK{i}", [P, 512], F32)) for i in range(2)]
            psR = [E1(nc.psum_tensor(f"psR{i}", [P, 512], F32)) for i in range(2)]

            r_wkT, r_wrT = Res("wkT"), Res("wrT")
            r_x_tok = [Res("x_tok0")]
            r_xn_tok = [Res("xn_tok0")]
            r_xn_fm = [Res(f"xn_fm{i}") for i in range(2)]
            r_outb = [Res("outb0")]
            r_S1 = [[Res(f"S1_{i}_{c}") for c in range(CC)] for i in range(2)]
            r_S3 = [Res(f"S3_{i}") for i in range(2)]
            r_S4 = [[Res(f"S4_{i}_{c}") for c in range(CC)] for i in range(2)]
            r_S5, r_S6 = Res("S5"), Res("S6")
            r_s5 = [[Res(f"s5_{i}_{c}") for c in range(CC)] for i in range(2)]
            r_s6 = [[Res(f"s6_{i}_{c}") for c in range(CC)] for i in range(2)]
            r_zNb = [Res(f"zNb{i}") for i in range(2)]
            r_zDb = [Res(f"zDb{i}") for i in range(2)]
            r_ones = Res("onesb")
            r_mv = [Res(f"mv{i}") for i in range(2)]
            r_stt = [Res(f"stt{i}") for i in range(2)]
            r_rstd = [Res(f"rstd{i}") for i in range(2)]
            r_lnv = [Res(f"lnv{i}") for i in range(2)]
            r_pp_in = [Res(f"pp_in{i}") for i in range(2)]
            r_psT = [Res(f"psT{i}") for i in range(2)]
            r_psP = [Res(f"psP{i}") for i in range(2)]
            r_psK = [Res(f"psK{i}") for i in range(2)]
            r_psR = [Res(f"psR{i}") for i in range(2)]
            r_krtok = [Res(f"krtok{i}") for i in range(2)]

            bd.op("sync", lambda e: e.dma_start(out=ident[:], in_=id_d.ap()),
                  writes=[r_ident], dma_sem="d_setup")
            bd.op("sync", lambda e: e.dma_start(out=tdv[:], in_=tdp_d.ap()),
                  writes=[r_tdv], dma_sem="d_setup")
            bd.op("sync", lambda e: e.dma_start(out=tfv[:], in_=tfp_d.ap()),
                  writes=[r_tfv], dma_sem="d_setup")
            for i, v in [(0, EPS), (1, -0.5), (2, -1.0), (5, 127.0)]:
                bd.op("gpsimd", lambda e, i=i, v=v: e.memset(cst[:, i:i + 1], v),
                      writes=[r_cst])
            bd.op("vector", lambda e: e.reciprocal(cst[:, 3:4], cst[:, 5:6]),
                  reads=[r_cst], writes=[r_cst])
            bd.op("vector", lambda e: e.tensor_tensor(
                out=cst[:, 4:5], in0=cst[:, 3:4], in1=cst[:, 3:4], op=ALU.mult),
                reads=[r_cst], writes=[r_cst])
            bd.op("scalar", lambda e: e.activation(out=euv[:], in_=tfv[:], func=AF.Exp),
                  reads=[r_tfv], writes=[r_euv])
            bd.op("scalar", lambda e: e.activation(out=wneg[:], in_=tdv[:], func=AF.Exp),
                  reads=[r_tdv], writes=[r_wneg])
            bd.op("scalar", lambda e: e.activation(out=dv[:], in_=wneg[:], func=AF.Exp,
                                                   scale=cst[:, 2:3]),
                  reads=[r_wneg, r_cst], writes=[r_dv])
            zN3 = [zNb[i][:].rearrange("p (c t) -> p c t", t=TT1 + 1) for i in range(2)]
            zD3 = [zDb[i][:].rearrange("p (c t) -> p c t", t=TT1 + 1) for i in range(2)]
            bd.op("vector", lambda e: e.memset(zN3[1][:, :, TT1:TT1 + 1], 0.0), writes=[r_zNb[1]])
            bd.op("vector", lambda e: e.memset(zD3[1][:, :, TT1:TT1 + 1], 0.0), writes=[r_zDb[1]])
            bd.op("vector", lambda e: e.memset(zN3[0][:, :, 0:1], 0.0), writes=[r_zNb[0]])
            bd.op("vector", lambda e: e.memset(zD3[0][:, :, 0:1], 0.0), writes=[r_zDb[0]])

            def pp_mat_unit(src_d, row_cc, dstT, r_dstT, nrowchunks=CC):
                sl = pp_mat_unit.ctr % 2
                pp_mat_unit.ctr += 1
                src_ap = dram_ap(src_d, row_cc * P * C, [[C, P], [1, C]])
                bd.op("gpsimd",
                      lambda e, sl=sl, src_ap=src_ap: e.dma_start(
                          out=pp_in[sl][:, 0:C], in_=src_ap),
                      writes=[r_pp_in[sl]], dma_sem=f"d_pp{sl}")

                def tr(e, sl=sl):
                    last = None
                    for cc in range(CC):
                        last = e.transpose(psP[sl][:, cc * P:(cc + 1) * P],
                                           pp_in[sl][:, cc * P:(cc + 1) * P], ident[:])
                    return last
                bd.op("tensor", tr, reads=[r_pp_in[sl], r_ident], writes=[r_psP[sl]])
                dst3 = dstT[:].rearrange("p (a b) -> p a b", b=C)
                bd.op("scalar",
                      lambda e, sl=sl, row_cc=row_cc: e.copy(
                          dst3[:, :, row_cc * P:(row_cc + 1) * P], psP[sl][:, 0:C]),
                      reads=[r_psP[sl]], writes=[r_dstT])
            pp_mat_unit.ctr = 0

            def pp_wgu_unit(src_d, hc, iu):
                sl = pp_mat_unit.ctr % 2
                pp_mat_unit.ctr += 1
                src_ap = dram_ap(src_d, hc * P * C, [[C, P], [1, C]])
                bd.op("gpsimd",
                      lambda e, sl=sl, src_ap=src_ap: e.dma_start(
                          out=pp_in[sl][:, 0:C], in_=src_ap),
                      writes=[r_pp_in[sl]], dma_sem=f"d_pp{sl}")

                def tr(e, sl=sl):
                    last = None
                    for cc in range(CC):
                        last = e.transpose(psP[sl][:, cc * P:(cc + 1) * P],
                                           pp_in[sl][:, cc * P:(cc + 1) * P], ident[:])
                    return last
                bd.op("tensor", tr, reads=[r_pp_in[sl], r_ident], writes=[r_psP[sl]])
                ssl = pp_mat_unit.ctr % 2
                bd.op("scalar",
                      lambda e, sl=sl, ssl=ssl: e.copy(pp_stage[ssl][:], psP[sl][:, 0:C]),
                      reads=[r_psP[sl]], writes=[r_pp_stage[ssl]])
                dst = dram_ap(wgu_s, ((hc * 2 + iu) * CC) * P * P,
                              [[P, P], [P * P, CC], [1, P]])
                bd.op("sync",
                      lambda e, ssl=ssl, dst=dst: e.dma_start(out=dst, in_=pp_stage[ssl][:]),
                      reads=[r_pp_stage[ssl]], writes=[r_wgu_s[hc]], dma_sem=f"d_pps{ssl}")

            def pp_wd_unit(dc):
                sl = 0
                pp_mat_unit.ctr += 1
                src_ap = dram_ap(wd_d, dc * P * H, [[H, P], [1, H]])
                bd.op("gpsimd",
                      lambda e, sl=sl, src_ap=src_ap: e.dma_start(
                          out=pp_in[sl][:, 0:H], in_=src_ap),
                      writes=[r_pp_in[sl]], dma_sem=f"d_pp{sl}")
                for g in range(3):
                    psl = (sl + g) % 2
                    def tr(e, sl=sl, g=g, psl=psl):
                        last = None
                        for j in range(CC):
                            hc = g * CC + j
                            last = e.transpose(psP[psl][:, j * P:(j + 1) * P],
                                               pp_in[sl][:, hc * P:(hc + 1) * P], ident[:])
                        return last
                    bd.op("tensor", tr, reads=[r_pp_in[sl], r_ident], writes=[r_psP[psl]])
                    ssl = (pp_mat_unit.ctr + g) % 2
                    bd.op("scalar",
                          lambda e, psl=psl, ssl=ssl: e.copy(pp_stage[ssl][:], psP[psl][:, 0:C]),
                          reads=[r_psP[psl]], writes=[r_pp_stage[ssl]])
                    dst = dram_ap(wdT_s, (g * CC) * P * C + dc * P,
                                  [[C, P], [P * C, CC], [1, P]])
                    bd.op("sync",
                          lambda e, ssl=ssl, dst=dst: e.dma_start(out=dst, in_=pp_stage[ssl][:]),
                          reads=[r_pp_stage[ssl]],
                          writes=[r_wdT_s[g * CC + j] for j in range(CC)],
                          dma_sem=f"d_pps{ssl}")

            pp_stage = [E1(nc.sbuf_tensor(f"pp_stage{i}", [P, C], BF16)) for i in range(2)]
            r_pp_stage = [Res(f"pp_stage{i}") for i in range(2)]


            for cc in range(CC):
                pp_mat_unit(wk_d, cc, wkT, r_wkT)
            for cc in range(CC):
                pp_mat_unit(wr_d, cc, wrT, r_wrT)

            pp_units = []
            for cc in range(CC):
                pp_units.append(lambda cc=cc: pp_mat_unit(wo_d, cc, woT, r_woT))
            for hc in range(HC):
                pp_units.append(lambda hc=hc: pp_wgu_unit(wg_d, hc, 0))
                pp_units.append(lambda hc=hc: pp_wgu_unit(wu_d, hc, 1))
            for dc in range(CC):
                pp_units.append(lambda dc=dc: pp_wd_unit(dc))
            pp_idx = 0

            def stage_a_dma(ch):
                t0 = ch * TT1
                sl = ch % 2
                src = dram_ap(x_d, t0 * C, [[C, P], [P * C, TS1], [1, C]])
                bd.op("sync", lambda e, src=src: e.dma_start(out=x_tok[0][:], in_=src),
                      writes=[r_x_tok[0]], dma_sem="d_x0")

            def stage_a_stats(ch):
                sl = ch % 2
                for ts in range(TS1):
                    for hh in range(2):
                        bd.op("vector",
                              lambda e, sl=sl, ts=ts, hh=hh: e.bn_stats(
                                  stt_b[sl][:, (ts * 2 + hh) * 6:(ts * 2 + hh + 1) * 6],
                                  x_tok[0][:, ts * C + hh * 512: ts * C + (hh + 1) * 512]),
                              reads=[r_x_tok[0]], writes=[r_stt[sl]])
                    bd.op("vector",
                          lambda e, sl=sl, ts=ts: e.bn_aggr(
                              mvb[sl][:, ts * 2:(ts + 1) * 2],
                              stt_b[sl][:, ts * 12:(ts * 12) + 12]),
                          reads=[r_stt[sl]], writes=[r_mv[sl]])
                    bd.op("scalar",
                          lambda e, sl=sl, ts=ts: e.activation(
                              out=lnv[sl][:, ts:ts + 1], in_=mvb[sl][:, ts * 2 + 1:ts * 2 + 2],
                              func=AF.Ln, bias=cst[:, 0:1]),
                          reads=[r_mv[sl], r_cst], writes=[r_lnv[sl]])
                    bd.op("scalar",
                          lambda e, sl=sl, ts=ts: e.activation(
                              out=rstd[sl][:, ts:ts + 1], in_=lnv[sl][:, ts:ts + 1],
                              func=AF.Exp, scale=cst[:, 1:2]),
                          reads=[r_lnv[sl], r_cst], writes=[r_rstd[sl]])

            def stage_a_apply(ch):
                sl = ch % 2
                for ts in range(TS1):
                    bd.op("vector",
                          lambda e, sl=sl, ts=ts: e.tensor_scalar(
                              out=xn_tok[0][:, ts * C:(ts + 1) * C],
                              in0=x_tok[0][:, ts * C:(ts + 1) * C],
                              scalar1=mvb[sl][:, ts * 2:ts * 2 + 1], op0=ALU.subtract,
                              scalar2=rstd[sl][:, ts:ts + 1], op1=ALU.mult),
                          reads=[r_x_tok[0], r_mv[sl], r_rstd[sl]], writes=[r_xn_tok[0]])

            def stage_a_post(ch):
                t0 = ch * TT1
                sl = ch % 2
                for ts in range(TS1):
                    def tr(e, sl=sl, ts=ts):
                        last = None
                        for cc in range(CC):
                            last = e.transpose(psT[ts][:, cc * P:(cc + 1) * P],
                                               xn_tok[0][:, ts * C + cc * P: ts * C + (cc + 1) * P],
                                               ident[:])
                        return last
                    bd.op("tensor", tr, reads=[r_xn_tok[0], r_ident], writes=[r_psT[ts]])
                    xnf3 = xn_fm[sl][:].rearrange("p (a b) -> p a b", b=TT1)
                    bd.op("scalar",
                          lambda e, sl=sl, ts=ts, xnf3=xnf3: e.copy(
                              xnf3[:, :, ts * P:(ts + 1) * P], psT[ts][:, 0:C]),
                          reads=[r_psT[ts]], writes=[r_xn_fm[sl]])
                dst = dram_ap(xn_s, t0, [[T, P], [P * T, CC], [1, TT1]])
                bd.op("sync", lambda e, sl=sl, dst=dst: e.dma_start(out=dst, in_=xn_fm[sl][:]),
                      reads=[r_xn_fm[sl]], writes=[r_xn_s[ch]], dma_sem=f"d_xns{sl}")

            def stage_b(ch):
                sl = ch % 2
                for pr in range(CC // 2):
                    pk = pr % 2
                    for half in range(2):
                        cc_o = pr * 2 + half
                        for (wT, r_wT, psx, r_psx) in ((wkT, r_wkT, psK, r_psK),
                                                       (wrT, r_wrT, psR, r_psR)):
                            def mm(e, sl=sl, cc_o=cc_o, wT=wT, psx=psx, pk=pk, half=half):
                                last = None
                                for cc_i in range(CC):
                                    last = e.matmul(
                                        psx[pk][:, half * TT1:(half + 1) * TT1],
                                        wT[:, cc_i * C + cc_o * P: cc_i * C + (cc_o + 1) * P],
                                        xn_fm[sl][:, cc_i * TT1:(cc_i + 1) * TT1],
                                        start=(cc_i == 0), stop=(cc_i == CC - 1),
                                        skip_group_check=True)
                                return last
                            bd.op("tensor", mm, reads=[r_wT, r_xn_fm[sl]], writes=[r_psx[pk]])
                    c0 = pr * 2
                    bd.op("scalar",
                          lambda e, sl=sl, c0=c0, pk=pk: e.activation(
                              out=S1[sl][:, c0 * TT1:(c0 + 2) * TT1],
                              in_=psK[pk][:], func=AF.Exp),
                          reads=[r_psK[pk]], writes=[r_S1[sl][c0], r_S1[sl][c0 + 1]])
                    bd.op("vector",
                          lambda e, sl=sl, c0=c0, pk=pk: e.tensor_tensor(
                              out=S4[sl][:, c0 * TT1:(c0 + 2) * TT1],
                              in0=S1[sl][:, c0 * TT1:(c0 + 2) * TT1],
                              in1=psR[pk][:], op=ALU.mult),
                          reads=[r_S1[sl][c0], r_S1[sl][c0 + 1], r_psR[pk]],
                          writes=[r_S4[sl][c0], r_S4[sl][c0 + 1], r_krtok[pk]])
                    bd.op("scalar",
                          lambda e, sl=sl, c0=c0, pk=pk: e.activation(
                              out=S3[sl][:, c0 * TT1:(c0 + 2) * TT1],
                              in_=psR[pk][:], func=AF.Exp, scale=cst[:, 2:3]),
                          reads=[r_psR[pk], r_cst, r_krtok[pk]], writes=[r_S3[sl]])
                if ch > 0:
                    ol = 1 - sl
                    bd.op("scalar",
                          lambda e, sl=sl, ol=ol: e.copy(zN3[sl][:, :, 0:1],
                                                         zN3[ol][:, :, TT1:TT1 + 1]),
                          reads=[r_zNb[ol]], writes=[r_zNb[sl]])
                    bd.op("scalar",
                          lambda e, sl=sl, ol=ol: e.copy(zD3[sl][:, :, 0:1],
                                                         zD3[ol][:, :, TT1:TT1 + 1]),
                          reads=[r_zDb[ol]], writes=[r_zDb[sl]])

            def stage_c_scans(ch, lo, hi):
                sl = ch % 2
                ol = 1 - sl
                for cc in range(lo, hi):
                    b0 = cc * (TT1 + 1)
                    bd.op("vector",
                          lambda e, sl=sl, ol=ol, cc=cc, b0=b0: e.tensor_tensor_scan(
                              out=zDb[sl][:, b0 + 1: b0 + 1 + TT1],
                              data0=bass.AP(dv, cc, [[CC, P], [0, TT1]]),
                              data1=S1[sl][:, cc * TT1:(cc + 1) * TT1],
                              initial=zDb[ol][:, b0 + TT1: b0 + TT1 + 1],
                              op0=ALU.mult, op1=ALU.add),
                          reads=[r_dv, r_S1[sl][cc], r_zDb[ol]], writes=[r_zDb[sl]])
                    bd.op("vector",
                          lambda e, sl=sl, ol=ol, cc=cc, b0=b0: e.tensor_tensor_scan(
                              out=zNb[sl][:, b0 + 1: b0 + 1 + TT1],
                              data0=bass.AP(dv, cc, [[CC, P], [0, TT1]]),
                              data1=S4[sl][:, cc * TT1:(cc + 1) * TT1],
                              initial=zNb[ol][:, b0 + TT1: b0 + TT1 + 1],
                              op0=ALU.mult, op1=ALU.add),
                          reads=[r_dv, r_S4[sl][cc], r_zNb[ol]], writes=[r_zNb[sl]])

            def stage_c_stt(ch):
                sl = ch % 2
                for cc in range(CC):
                    sli = slice(cc * TT1, (cc + 1) * TT1)
                    bd.op("vector",
                          lambda e, sl=sl, cc=cc, sli=sli: e.scalar_tensor_tensor(
                              out=S5[sl][:, sli], in0=S1[sl][:, sli],
                              scalar=euv[:, cc:cc + 1],
                              in1=zD3[sl][:, cc, 0:TT1], op0=ALU.mult, op1=ALU.add),
                          reads=[r_S1[sl][cc], r_euv, r_zDb[sl]], writes=[r_s5[sl][cc]])
                    bd.op("vector",
                          lambda e, sl=sl, sli=sli, cc=cc: e.scalar_tensor_tensor(
                              out=S6[sl][:, sli], in0=S3[sl][:, sli], scalar=1.0,
                              in1=S5[sl][:, sli], op0=ALU.add, op1=ALU.mult),
                          reads=[r_S3[sl], r_s5[sl][cc]], writes=[r_s6[sl][cc]])
            def stage_d(ch):
                sl = ch % 2
                for cc in range(CC):
                    sli = slice(cc * TT1, (cc + 1) * TT1)
                    bd.op("scalar",
                          lambda e, sl=sl, sli=sli: e.activation(
                              out=S5[sl][:, sli], in_=S6[sl][:, sli], func=AF.Ln),
                          reads=[r_s6[sl][cc]], writes=[r_s5[sl][cc]])
                    bd.op("scalar",
                          lambda e, sl=sl, sli=sli: e.activation(
                              out=S6[sl][:, sli], in_=S5[sl][:, sli],
                              func=AF.Exp, scale=cst[:, 2:3]),
                          reads=[r_s5[sl][cc], r_cst], writes=[r_s6[sl][cc]])

            def stage_e(ch):
                t0 = ch * TT1
                sl = ch % 2
                for cc in range(CC):
                    bd.op("vector",
                          lambda e, sl=sl, cc=cc: e.scalar_tensor_tensor(
                              out=S5[sl][:, cc * TT1:(cc + 1) * TT1],
                              in0=S4[sl][:, cc * TT1:(cc + 1) * TT1],
                              scalar=euv[:, cc:cc + 1],
                              in1=zN3[sl][:, cc, 0:TT1], op0=ALU.mult, op1=ALU.add),
                          reads=[r_S4[sl][cc], r_euv, r_zNb[sl]],
                          writes=[r_s5[sl][cc]])
                bd.op("gpsimd", lambda e, sl=sl: e.tensor_tensor(
                    out=outb[0][:], in0=S5[sl][:], in1=S6[sl][:], op=ALU.mult),
                    reads=[r_s5[sl][cc] for cc in range(CC)]
                          + [r_s6[sl][cc] for cc in range(CC)],
                    writes=[r_outb[0]])
                dst = dram_ap(outp_s, t0, [[T, P], [P * T, CC], [1, TT1]])
                bd.op("sync", lambda e, dst=dst: e.dma_start(out=dst, in_=outb[0][:]),
                      reads=[r_outb[0]], writes=[r_out_s[ch]], dma_sem="d_ots0")

            stage_a_dma(0)
            stage_a_stats(0)
            stage_a_apply(0)
            stage_a_post(0)
            stage_b(0)
            stage_a_dma(1)
            stage_a_stats(1)
            stage_a_apply(1)
            stage_a_post(1)
            for ch in range(NC1):
                if ch + 2 < NC1:
                    stage_a_dma(ch + 2)
                stage_c_scans(ch, 0, CC)
                stage_c_stt(ch)
                if ch > 0:
                    stage_e(ch - 1)
                if ch + 1 < NC1:
                    stage_b(ch + 1)
                if ch + 2 < NC1:
                    stage_a_stats(ch + 2)
                    stage_a_apply(ch + 2)
                    stage_a_post(ch + 2)
                stage_d(ch)
                n_units = (len(pp_units) * (ch + 1)) // NC1
                while pp_idx < n_units:
                    pp_units[pp_idx]()
                    pp_idx += 1
            stage_e(NC1 - 1)

            while pp_idx < len(pp_units):
                pp_units[pp_idx]()
                pp_idx += 1

            bd.finalize(block)

        with contextlib.ExitStack() as ph2:
            E2 = ph2.enter_context
            xnc = [E2(nc.sbuf_tensor(f"xnc{i}", [P, CC * TT2], BF16)) for i in range(2)]
            otc = [E2(nc.sbuf_tensor(f"otc{i}", [P, CC * TT2], BF16)) for i in range(2)]
            x2 = [E2(nc.sbuf_tensor(f"x2_{i}", [P, TS2 * C], F32)) for i in range(2)]
            res_t = [E2(nc.sbuf_tensor(f"res{i}", [P, TS2 * C], F32)) for i in range(2)]
            h_sb = E2(nc.sbuf_tensor("h_sb", [P, HC * TT2], BF16))
            wgu_t = [E2(nc.sbuf_tensor(f"wgu{i}", [P, 2 * CC * P], BF16)) for i in range(4)]
            wd_t = [E2(nc.sbuf_tensor(f"wdt{i}", [P, C], BF16)) for i in range(4)]
            sil_b = [E2(nc.sbuf_tensor(f"sil{i}", [P, TT2], F32)) for i in range(2)]

            psA = [E2(nc.psum_tensor(f"psA{i}", [P, TT2], F32)) for i in range(2)]
            psB = [E2(nc.psum_tensor(f"psB{i}", [P, TT2], F32)) for i in range(2)]
            psRb = [E2(nc.psum_tensor(f"psRb{i}", [P, TT2], F32)) for i in range(4)]

            r_xnc = [Res(f"xnc{i}") for i in range(2)]
            r_otc = [Res(f"otc{i}") for i in range(2)]
            r_x2 = [Res(f"x2_{i}") for i in range(2)]
            r_res = [Res(f"res{i}") for i in range(2)]
            r_h = Res("h_sb")
            r_wgu = [Res(f"wgu{i}") for i in range(4)]
            r_wd = [Res(f"wdt{i}") for i in range(4)]
            r_sil = [Res(f"sil{i}") for i in range(2)]
            r_psA = [Res(f"psA{i}") for i in range(2)]
            r_psB = [Res(f"psB{i}") for i in range(2)]
            r_psRb = [Res(f"psRb{i}") for i in range(4)]
            banks = [r_psRb[0], r_psRb[1], r_psRb[2], r_psRb[3],
                     r_psA[0], r_psA[1], r_psB[0], r_psB[1]]
            bank_t = [psRb[0], psRb[1], psRb[2], psRb[3], psA[0], psA[1], psB[0], psB[1]]

            bd.fence()

            for ch in range(NC2):
                t0 = ch * TT2
                sl = ch % 2
                src = dram_ap(xn_s, t0, [[T, P], [P * T, CC], [1, TT2]])
                bd.op("sync", lambda e, sl=sl, src=src: e.dma_start(out=xnc[sl][:], in_=src),
                      reads=[r_xn_s[2 * ch], r_xn_s[2 * ch + 1]],
                      writes=[r_xnc[sl]], dma_sem=f"d_xnc{sl}")
                src = dram_ap(outp_s, t0, [[T, P], [P * T, CC], [1, TT2]])
                bd.op("sync", lambda e, sl=sl, src=src: e.dma_start(out=otc[sl][:], in_=src),
                      reads=[r_out_s[2 * ch], r_out_s[2 * ch + 1]],
                      writes=[r_otc[sl]], dma_sem=f"d_otc{sl}")
                src = dram_ap(x_d, t0 * C, [[C, P], [P * C, TS2], [1, C]])
                bd.op("sync", lambda e, sl=sl, src=src: e.dma_start(out=x2[sl][:], in_=src),
                      writes=[r_x2[sl]], dma_sem=f"d_x2{sl}")

                for hc in range(HC):
                    wsl = hc % 4
                    src = dram_ap(wgu_s, hc * 2 * CC * P * P,
                                  [[P, P], [CC * P * P, 2], [P * P, CC], [1, P]])
                    bd.op("sync",
                          lambda e, wsl=wsl, src=src: e.dma_start(out=wgu_t[wsl][:], in_=src),
                          reads=[r_wgu_s[hc]], writes=[r_wgu[wsl]], dma_sem=f"d_wgu{wsl}")

                    pab = hc % 2
                    def mmg(e, sl=sl, wsl=wsl, pab=pab, iu=0):
                        last = None
                        for cc in range(CC):
                            last = e.matmul(
                                psA[pab][:],
                                wgu_t[wsl][:, (iu * CC + cc) * P:(iu * CC + cc + 1) * P],
                                xnc[sl][:, cc * TT2:(cc + 1) * TT2],
                                start=(cc == 0), stop=(cc == CC - 1))
                        return last
                    bd.op("tensor", mmg, reads=[r_wgu[wsl], r_xnc[sl]], writes=[r_psA[pab]])

                    def mmu(e, sl=sl, wsl=wsl, pab=pab, iu=1):
                        last = None
                        for cc in range(CC):
                            last = e.matmul(
                                psB[pab][:],
                                wgu_t[wsl][:, (iu * CC + cc) * P:(iu * CC + cc + 1) * P],
                                xnc[sl][:, cc * TT2:(cc + 1) * TT2],
                                start=(cc == 0), stop=(cc == CC - 1))
                        return last
                    bd.op("tensor", mmu, reads=[r_wgu[wsl], r_xnc[sl]], writes=[r_psB[pab]])
                    bd.op("scalar",
                          lambda e, pab=pab: e.activation(
                              out=sil_b[pab][:], in_=psA[pab][:], func=AF.Silu,
                              scale=cst[:, 3:4]),
                          reads=[r_psA[pab], r_cst], writes=[r_sil[pab]])
                    bd.op("vector",
                          lambda e, hc=hc, pab=pab: e.scalar_tensor_tensor(
                              out=h_sb[:, hc * TT2:(hc + 1) * TT2],
                              in0=sil_b[pab][:], scalar=cst[:, 4:5], in1=psB[pab][:],
                              op0=ALU.mult, op1=ALU.mult),
                          reads=[r_sil[pab], r_psB[pab]], writes=[r_h])

                for hc in range(HC):
                    wsl = hc % 4
                    src = dram_ap(wdT_s, hc * P * C, [[C, P], [1, C]])
                    bd.op("sync",
                          lambda e, wsl=wsl, src=src: e.dma_start(out=wd_t[wsl][:], in_=src),
                          reads=[r_wdT_s[hc]], writes=[r_wd[wsl]], dma_sem=f"d_wd{wsl}")

                    def mmd(e, hc=hc, wsl=wsl):
                        last = None
                        for ts in range(TS2):
                            for dh in range(2):
                                last = e.matmul(
                                    bank_t[ts * 2 + dh][:],
                                    h_sb[:, hc * TT2 + ts * P: hc * TT2 + (ts + 1) * P],
                                    wd_t[wsl][:, dh * 512:(dh + 1) * 512],
                                    start=(hc == 0), stop=False, skip_group_check=True)
                        return last
                    bd.op("tensor", mmd, reads=[r_h, r_wd[wsl]], writes=banks)
                for hc2 in range(CC):
                    def mmw(e, sl=sl, hc2=hc2):
                        last = None
                        for ts in range(TS2):
                            for dh in range(2):
                                last = e.matmul(
                                    bank_t[ts * 2 + dh][:],
                                    otc[sl][:, hc2 * TT2 + ts * P: hc2 * TT2 + (ts + 1) * P],
                                    woT[:, hc2 * C + dh * 512: hc2 * C + (dh + 1) * 512],
                                    start=False, stop=(hc2 == CC - 1), skip_group_check=True)
                        return last
                    bd.op("tensor", mmw, reads=[r_otc[sl], r_woT], writes=banks)
                for grp in range(2 * TS2):
                    bd.op("vector",
                          lambda e, sl=sl, grp=grp: e.tensor_tensor(
                              out=res_t[sl][:, grp * 512:(grp + 1) * 512],
                              in0=bank_t[grp][:],
                              in1=x2[sl][:, grp * 512:(grp + 1) * 512], op=ALU.add),
                          reads=[banks[grp], r_x2[sl]], writes=[r_res[sl]])
                dst = dram_ap(out_d, t0 * C, [[C, P], [P * C, TS2], [1, C]])
                bd.op("gpsimd", lambda e, sl=sl, dst=dst: e.dma_start(out=dst, in_=res_t[sl][:]),
                      reads=[r_res[sl]], dma_sem=f"d_rs{sl}")

            bd.fence(("sync", "gpsimd"))
            bd.finalize(block)

    return nc


_PROG_CACHE = {}


def kernel(L0_attn_x, L0_attn_Wk, L0_attn_Wr, L0_attn_Wo,
           L0_attn_time_decay, L0_attn_time_first,
           L0_ffn_w_gate, L0_ffn_w_up, L0_ffn_w_down):
    x = np.ascontiguousarray(np.asarray(L0_attn_x, dtype=np.float32))
    B, T, C_ = x.shape
    wk = np.ascontiguousarray(np.asarray(L0_attn_Wk, dtype=np.float32))
    wr = np.ascontiguousarray(np.asarray(L0_attn_Wr, dtype=np.float32))
    wo = np.ascontiguousarray(np.asarray(L0_attn_Wo, dtype=np.float32))
    wg = np.ascontiguousarray(np.asarray(L0_ffn_w_gate, dtype=np.int32))
    wu = np.ascontiguousarray(np.asarray(L0_ffn_w_up, dtype=np.int32))
    wd = np.ascontiguousarray(np.asarray(L0_ffn_w_down, dtype=np.int32))
    td = np.asarray(L0_attn_time_decay, dtype=np.float32)
    tf = np.asarray(L0_attn_time_first, dtype=np.float32)
    tdp = np.ascontiguousarray(td.reshape(CC, P).T)
    tfp = np.ascontiguousarray(tf.reshape(CC, P).T)
    ident = np.eye(P, dtype=ml_dtypes.bfloat16)

    if T not in _PROG_CACHE:
        _PROG_CACHE[T] = build_program(T)
    nc = _PROG_CACHE[T]

    in_maps = []
    for b in range(B):
        in_maps.append(dict(x=np.ascontiguousarray(x[b]), Wk=wk, Wr=wr, Wo=wo,
                            wg=wg, wu=wu, wd=wd, tdp=tdp, tfp=tfp, ident=ident))
    res = run_bass_kernel_spmd(nc, in_maps, core_ids=list(range(B)))
    out = np.stack([np.asarray(r["out"], dtype=np.float32) for r in res.results], axis=0)
    return out
